# version 01
# baseline (speedup 1.0000x reference)
from contextlib import ExitStack

import numpy as np
import ml_dtypes
import concourse.bass as bass
import concourse.mybir as mybir
from concourse.bass_utils import run_bass_kernel_spmd

F32 = mybir.dt.float32
BF16 = mybir.dt.bfloat16
AX = mybir.AxisListType.X
AF = mybir.ActivationFunctionType
OP = mybir.AluOpType
BF = ml_dtypes.bfloat16

T = 2048
D = 1024
DT = 8
TT = 16
NCH = 4
CH = 512
NEG = -1e30
HD = 64
NH_A = 16
STRIDE = 4
LOCAL = 8
NH_B = 8
INNER = 512
WINDOW = 512
HID = 2730
QK_SCALE = 8.0
LN_EPS = 1e-5
PH_OFF = (0, 1408)
PH_N = (1408, HID - 1408)
PH_TILES = (11, 11)
PH_LAST = (128, 42)


def cdiv(a, b):
    return (a + b - 1) // b


class StopEmission(Exception):
    pass


class MK:
    def __init__(self, debug=()):
        self.nc = bass.Bass("TRN2", target_bir_lowering=False, debug=False)
        self.stack = ExitStack()
        self.ops = {e: [] for e in ("t", "v", "s", "y", "g")}
        self.cnt = {e: 0 for e in ("t", "v", "s")}
        self.dma_cnt = {"LG": 0, "LY": 0, "ST": 0}
        self.sem = {}
        for name in ("T", "V", "S", "LG", "LY", "ST"):
            self.sem[name] = self.stack.enter_context(self.nc.semaphore(name))
        self.esem = {"t": "T", "v": "V", "s": "S"}
        self.waited = {e: {} for e in ("t", "v", "s", "y", "g")}
        self.bank = {b: {} for b in range(8)}
        self.debug = set(debug)
        self.dbg_names = []
        self.inputs = {}

    def _wait(self, eng, semname, val):
        if val <= 0:
            return
        if self.waited[eng].get(semname, -1) >= val:
            return
        self.waited[eng][semname] = val
        s = self.sem[semname]
        self.ops[eng].append(lambda e, s=s, v=val: e.wait_ge(s, v))

    def step(self, eng, fns, deps=(), banks=()):
        for semname, val in deps:
            if semname != self.esem[eng]:
                self._wait(eng, semname, val)
        for b in set(banks):
            for other, cval in list(self.bank[b].items()):
                if other != eng:
                    self._wait(eng, self.esem[other], cval)
        if not isinstance(fns, (list, tuple)):
            fns = [fns]
        self.cnt[eng] += 1
        my = self.cnt[eng]
        sem = self.sem[self.esem[eng]]
        guard = eng in ("v", "s")
        for i, fn in enumerate(fns):
            if guard:
                self.ops[eng].append(lambda e: e.nop())
            if i == len(fns) - 1:
                self.ops[eng].append(lambda e, fn=fn, sem=sem: fn(e).then_inc(sem, 1))
            else:
                self.ops[eng].append(fn)
        for b in set(banks):
            self.bank[b][eng] = my
        return (self.esem[eng], my)

    def dma(self, queue, out, in_, deps=(), store=False):
        semname = "ST" if store else {"g": "LG", "y": "LY"}[queue]
        for dsem, val in deps:
            self._wait(queue, dsem, val)
        self.dma_cnt[semname] += 16
        s = self.sem[semname]
        self.ops[queue].append(
            lambda e, out=out, in_=in_, s=s: e.dma_start(out=out, in_=in_).then_inc(s, 16))
        return (semname, self.dma_cnt[semname])

    def input(self, name, arr, sbuf=None, queue="g", deps=()):
        arr = np.ascontiguousarray(arr)
        dt = {np.dtype(np.float32): F32, np.dtype(BF): BF16}[arr.dtype]
        h = self.nc.dram_tensor(name, list(arr.shape), dt, kind="ExternalInput")
        self.inputs[name] = arr
        if sbuf is None:
            return h.ap()
        return self.dma(queue, sbuf, h.ap(), deps=deps)

    def dump(self, name, ap, deps=()):
        if name not in self.debug:
            return
        h = self.nc.dram_tensor("dbg_" + name, list(ap.shape), ap.dtype,
                                kind="ExternalOutput")
        self.dbg_names.append(name)
        self.dma("y", h.ap(), ap, deps=deps, store=True)

    def finish(self):
        st = self.dma_cnt["ST"]
        s = self.sem["ST"]
        self.ops["y"].append(lambda e, s=s, v=st: e.wait_ge(s, v))
        with self.nc.Block() as block:
            @block.tensor
            def _(t):
                for f in self.ops["t"]:
                    f(t)

            @block.vector
            def _(v):
                for f in self.ops["v"]:
                    f(v)

            @block.scalar
            def _(s_):
                for f in self.ops["s"]:
                    f(s_)

            @block.sync
            def _(y):
                for f in self.ops["y"]:
                    f(y)

            @block.gpsimd
            def _(g):
                for f in self.ops["g"]:
                    f(g)


class Buf:
    def __init__(self, ap_fn, tiles, cols, dtype):
        self.ap_fn = ap_fn
        self.tiles = tiles
        self.cols = cols
        self.dtype = dtype

    def t(self, i):
        return self.ap_fn(i)

    def full(self):
        return self.ap_fn(0, span=self.tiles)


class Arena:
    def __init__(self, nc, name, nbytes):
        self.h = nc.alloc_sbuf_tensor(name, [128, nbytes], mybir.dt.uint8)
        self.ap = self.h.ap()
        self.nbytes = nbytes
        self.off = 0
        self.marks = []

    def _mk(self, base, tiles, cols, dtype):
        esz = 2 if dtype == BF16 else 4
        per = cols * esz

        def ap_fn(i, span=1, base=base, per=per, dtype=dtype):
            return self.ap[:, base + i * per: base + (i + span) * per].bitcast(dtype)
        return Buf(ap_fn, tiles, cols, dtype)

    def alloc(self, tiles, cols, dtype):
        esz = 2 if dtype == BF16 else 4
        self.off = (self.off + 3) & ~3
        base = self.off
        self.off += tiles * cols * esz
        assert self.off <= self.nbytes, f"arena overflow {self.off} > {self.nbytes}"
        return self._mk(base, tiles, cols, dtype)

    def at(self, byte_off, tiles, cols, dtype):
        esz = 2 if dtype == BF16 else 4
        assert byte_off % 4 == 0
        assert byte_off + tiles * cols * esz <= self.nbytes
        return self._mk(byte_off, tiles, cols, dtype)

    def mark(self):
        self.marks.append(self.off)

    def reset(self):
        self.off = self.marks.pop()


def host_consts():
    c = {}
    c["ident"] = np.eye(128, dtype=np.float32).astype(BF)
    c["identf"] = np.eye(128, dtype=np.float32)
    il = np.arange(128)[None, :]
    jl = np.arange(128)[:, None]
    ok = (jl % STRIDE != 0)
    m_lo = np.where((np.abs(il - jl + 128) <= LOCAL) & ok, 0.0, NEG)
    m_mid = np.where((np.abs(il - jl) <= LOCAL) & ok, 0.0, NEG)
    m_hi2 = np.where((np.abs(il - jl - 128) <= LOCAL) & ok, 0.0, NEG)
    full = np.full((128, 128), NEG)
    norm3 = np.concatenate([m_lo, m_mid, m_hi2], 1)
    first3 = np.concatenate([full, m_mid, m_hi2], 1)
    last3 = np.concatenate([m_lo, m_mid, full], 1)
    c["maskA_cat"] = np.concatenate([norm3, norm3], 1).astype(BF)
    c["maskA_first"] = np.concatenate([first3, norm3], 1).astype(BF)
    c["maskA_last"] = np.concatenate([norm3, last3], 1).astype(BF)
    mB_lo = np.where(jl >= il, 0.0, NEG)
    mB_hi = np.where(jl <= il, 0.0, NEG)
    c["maskB_cat"] = np.concatenate([mB_lo, mB_hi], 1).astype(BF)
    inv_freq = 1.0 / 10000.0 ** (2.0 * np.arange(32, dtype=np.float64) / HD)
    freq = inv_freq[np.concatenate([np.arange(32), np.arange(32)])]
    t = np.arange(T, dtype=np.float64)
    theta = freq[:, None] * t[None, :]
    base = (2.0 * np.arange(32, dtype=np.float64) + 0.4 * HD) / (1.4 * HD)
    base = base[np.concatenate([np.arange(32), np.arange(32)])]
    power = (t - T // 2) / float(WINDOW // 2)
    scale = base[:, None] ** power[None, :]
    cos, sin = np.cos(theta), np.sin(theta)
    two = lambda a: np.concatenate([a, a], 0)
    c["CQ8"] = two(QK_SCALE * cos * scale).astype(BF)
    c["SQ8"] = two(QK_SCALE * sin * scale).astype(BF)
    c["CK"] = two(cos / scale).astype(BF)
    c["SK"] = two(sin / scale).astype(BF)
    R = np.zeros((128, 128), dtype=np.float32)
    for h0 in (0, 64):
        for dd in range(32):
            R[h0 + dd, h0 + dd + 32] = -1.0
            R[h0 + dd + 32, h0 + dd] = 1.0
    c["rotT"] = R.T.copy().astype(BF)
    Q = np.zeros((128, 8, 16), dtype=np.float32)
    for j in range(8):
        Q[0:64, j, 2 * j] = 1.0
        Q[64:128, j, 2 * j + 1] = 1.0
    c["sumsq_lhsT"] = Q.reshape(128, 128).astype(BF)
    Bc = np.zeros((16, 8, 128), dtype=np.float32)
    for j in range(8):
        Bc[2 * j, j, 0:64] = 1.0
        Bc[2 * j + 1, j, 64:128] = 1.0
    c["bcast_lhsT"] = Bc.reshape(16, 1024)
    c["ones_col"] = np.ones((128, 1), dtype=BF)
    sk = np.zeros((16, 16, 64), dtype=np.float32)
    for h in range(16):
        sk[h, h, :] = 1.0
    c["selk16"] = sk.reshape(16, 1024)
    ls = np.zeros((16, 16, 128), dtype=np.float32)
    for cc in range(16):
        ls[cc, cc, :] = 1.0
    c["lnsel"] = ls.reshape(16, 2048)
    ns = np.zeros((8, 8, 128), dtype=np.float32)
    for h in range(8):
        ns[h, h, :] = -1.0
    c["negsel8"] = ns.reshape(8, 1024).astype(BF)
    c["one1f"] = np.ones((1, 1), dtype=np.float32)

    c["ones_r128f"] = np.ones((1, 128), dtype=np.float32)
    c["negones_r128"] = np.full((1, 128), -1.0, dtype=BF)
    return c


def tile_km(w):
    K, M = w.shape
    nt = cdiv(K, 128)
    out = np.zeros((128, nt * M), dtype=w.dtype)
    for k in range(nt):
        rows = w[k * 128:(k + 1) * 128]
        out[:rows.shape[0], k * M:(k + 1) * M] = rows
    return out


def tile_vec(v):
    n = cdiv(len(v), 128)
    out = np.zeros((128, n), dtype=v.dtype)
    for j in range(n):
        seg = v[j * 128:(j + 1) * 128]
        out[:len(seg), j] = seg
    return out


class Prog:
    def __init__(self, inputs, debug=(), stages=("A", "B0", "B1", "F"), stop_after=None):
        self.mk = MK(debug=debug)
        self.stages = stages
        self.stop_after = stop_after
        self.C = host_consts()
        self.IN = inputs
        nc = self.mk.nc

        self.x_in = nc.dram_tensor("x_fm", [128, DT * T], F32, kind="ExternalInput")
        self.out_h = nc.dram_tensor("out_fm", [128, DT * T], F32, kind="ExternalOutput")
        self.spill_x = nc.dram_tensor("spill_x", [128, DT * T], F32, kind="Internal")
        self.spill_b = nc.dram_tensor("spill_b", [128, DT * T], F32, kind="Internal")
        self.att_sp = nc.dram_tensor("att_sp", [128, DT * T], BF16, kind="Internal")

        sb = nc.alloc_sbuf_tensor
        self.ident = sb("ident", [128, 128], BF16).ap()
        self.identf = sb("identf", [128, 128], F32).ap()
        self.maskA_cat = sb("maskA_cat", [128, 768], BF16).ap()
        self.maskA_first = sb("maskA_first", [128, 768], BF16).ap()
        self.maskA_last = sb("maskA_last", [128, 768], BF16).ap()
        self.maskB_cat = sb("maskB_cat", [128, 256], BF16).ap()
        self.rotT = sb("rotT", [128, 128], BF16).ap()
        self.sumsq_lhsT = sb("sumsq_lhsT", [128, 128], BF16).ap()
        self.bcast_lhsT = sb("bcast_lhsT", [16, 1024], F32).ap()
        self.ones_col = sb("ones_col", [128, 1], BF16).ap()
        self.one1f = sb("one1f", [1, 1], F32).ap()

        self.ones_r128f = sb("ones_r128f", [1, 128], F32).ap()
        self.negones_r128 = sb("negones_r128", [1, 128], BF16).ap()
        self.c_ap = sb("c_ap", [128, 2], F32).ap()
        self.den16 = sb("den16", [16, T], F32).ap()
        self.r16f = sb("r16f", [16, T], F32).ap()
        self.selk16 = sb("selk16", [16, 1024], F32).ap()
        self.lnsel = sb("lnsel", [16, 2048], F32).ap()
        self.negsel8 = sb("negsel8", [8, 1024], BF16).ap()
        self.denstage = [sb(f"denstage{i}", [1, CH], F32).ap() for i in range(2)]
        self.ln_tm = sb("ln_tm", [128, 112], F32).ap()
        self.ln_rows = sb("ln_rows", [16, 256], F32).ap()
        self.stat_rows = sb("stat_rows", [1, 1024], F32).ap()
        self.ln_tmp = sb("ln_tmp", [128, CH], F32).ap()
        self.mrow = sb("mrow", [8, 256], BF16).ap()
        self.m8 = sb("m8", [128, 16], BF16).ap()
        self.ob_t = sb("ob_t", [128, 8], F32).ap()
        self.qs_t = sb("qs_t", [128, 2], F32).ap()
        self.ks_t = sb("ks_t", [128, 2], F32).ap()
        self.lng_t = sb("lng_t", [128, 16], F32).ap()
        self.lnb_t = sb("lnb_t", [128, 16], F32).ap()
        self.b1a_t = sb("b1a_t", [128, 44], F32).ap()
        self.b1g_t = sb("b1g_t", [128, 44], F32).ap()
        self.b2_t = sb("b2_t", [128, 16], F32).ap()
        self.unity = sb("unity", [128, 1], F32).ap()

        self.PS = nc.alloc_psum_tensor("PS", [128, 4096], F32).ap()
        self.arena = Arena(nc, "arena", 157696)
        self.x = self.arena.at(0, DT, T, F32)
        self.arena.off = DT * T * 4

        mki = self.mk.input
        IN = inputs
        self.h_qw = mki("w_qw", tile_km(IN["L0_attn_q_w"]))
        self.h_kw = mki("w_kw", tile_km(IN["L0_attn_k_w"]))
        self.h_vw = mki("w_vw", tile_km(IN["L0_attn_v_w"]))
        self.h_ow = mki("w_ow", tile_km(IN["L0_attn_o_w"]))
        self.h_wqkv = [mki(f"w_wqkv{l}", tile_km(IN["L0_ffn_wqkv"][l])) for l in range(2)]
        self.h_wo = [mki(f"w_wo{l}", tile_km(IN["L0_ffn_wo"][l])) for l in range(2)]
        w1, w2 = IN["L0_ffn_w1"], IN["L0_ffn_w2"]
        self.h_w1a = [mki(f"w_w1a{l}", tile_km(w1[l][:, 0:HID])) for l in range(2)]
        self.h_w1g = [mki(f"w_w1g{l}", tile_km(w1[l][:, HID:])) for l in range(2)]
        self.h_w2 = [mki(f"w_w2{l}", tile_km(w2[l])) for l in range(2)]
        ct = host_consts()
        self.h_rot = {nm: mki("c_" + nm, ct[nm]) for nm in ("CQ8", "SQ8", "CK", "SK")}

    def bankap(self, b, rows=128, cols=512, off=0):
        return self.PS[0:rows, 512 * b + off: 512 * b + off + cols]

    def dumpb(self, name, buf, deps):
        ap = buf.full() if isinstance(buf, Buf) else buf
        self.mk.dump(name, ap, deps=deps)

    def emit(self):
        mk = self.mk
        C = self.C
        self.tok_x = mk.dma("y", self.x.full(), self.x_in.ap())
        for nm, dst in (("ident", self.ident), ("identf", self.identf),
                        ("maskA_cat", self.maskA_cat), ("maskA_first", self.maskA_first),
                        ("maskA_last", self.maskA_last),
                        ("maskB_cat", self.maskB_cat),
                        ("rotT", self.rotT), ("sumsq_lhsT", self.sumsq_lhsT),
                        ("bcast_lhsT", self.bcast_lhsT), ("ones_col", self.ones_col),
                        ("one1f", self.one1f), ("selk16", self.selk16),
                        ("lnsel", self.lnsel), ("negsel8", self.negsel8),
                        ("ones_r128f", self.ones_r128f), ("negones_r128", self.negones_r128)):
            mk.input("c_" + nm, C[nm], sbuf=dst)
        IN = self.IN
        mk.input("v_ob", tile_vec(IN["L0_attn_o_b"]), sbuf=self.ob_t)
        mk.input("v_unity", np.full((128, 1), np.float32(IN["L0_attn_unity_scale"]),
                                    dtype=np.float32), sbuf=self.unity)
        mk.input("v_qs", np.stack([np.tile(IN["L0_ffn_q_scale"][l], 2) for l in range(2)],
                                  1).astype(np.float32), sbuf=self.qs_t)
        mk.input("v_ks", np.stack([np.tile(IN["L0_ffn_k_scale"][l], 2) for l in range(2)],
                                  1).astype(np.float32), sbuf=self.ks_t)
        mk.input("v_lng", np.concatenate([tile_vec(IN["L0_ffn_ln_g"][l]) for l in range(2)], 1),
                 sbuf=self.lng_t)
        mk.input("v_lnb", np.concatenate([tile_vec(IN["L0_ffn_ln_b"][l]) for l in range(2)], 1),
                 sbuf=self.lnb_t)
        mk.input("v_b1a", np.concatenate([tile_vec(IN["L0_ffn_b1"][l][0:HID]) for l in range(2)], 1),
                 sbuf=self.b1a_t)
        mk.input("v_b1g", np.concatenate([tile_vec(IN["L0_ffn_b1"][l][HID:]) for l in range(2)], 1),
                 sbuf=self.b1g_t)
        mk.input("v_b2", np.concatenate([tile_vec(IN["L0_ffn_b2"][l]) for l in range(2)], 1),
                 sbuf=self.b2_t)
        self.tok_cons = ("LG", mk.dma_cnt["LG"])

        self.tok_c = mk.step("s", [
            lambda e: e.activation(self.c_ap[:, 1:2], self.unity, AF.Sigmoid),
            lambda e: e.mul(self.c_ap[:, 0:1], self.c_ap[:, 1:2], 0.125),
        ], deps=[self.tok_cons])

        self.bankrot = [0]
        self.tok_last = [self.tok_x]
        try:
            if "A" in self.stages:
                self.emit_stageA()
            for l in range(2):
                if f"B{l}" in self.stages:
                    self.emit_stageB_layer(l)
            if "F" in self.stages:
                self.emit_final()
            else:
                mk.dma("y", self.out_h.ap(), self.x.full(),
                       deps=self.tok_last, store=True)
        except StopEmission:
            pass
        mk.finish()

    def stop_check(self, tag):
        if self.stop_after == tag:
            raise StopEmission()

    def nb(self, lo=0, hi=5):
        self.bankrot[0] = self.bankrot[0] + 1
        return lo + (self.bankrot[0] % (hi - lo))

    def emit_ln(self, src, dep_src, zc, sq, shift=False, gamma=None, beta=None,
                out_bf=None):
        mk = self.mk
        PS = self.PS

        def cast_chunk(ch, deps):
            fns = []
            for kt in range(DT):
                if not shift or kt < 4:
                    fns.append(lambda e, kt=kt, ch=ch: e.copy(
                        zc.t(kt), src.t(kt)[:, ch * CH:(ch + 1) * CH]))
                else:
                    if ch == 0:
                        fns.append(lambda e, kt=kt: e.copy(
                            zc.t(kt)[:, 1:CH], src.t(kt)[:, 0:CH - 1]))
                    else:
                        fns.append(lambda e, kt=kt, ch=ch: e.copy(
                            zc.t(kt), src.t(kt)[:, ch * CH - 1:(ch + 1) * CH - 1]))
            tok = mk.step("s", fns, deps=deps)
            if shift and ch == 0:
                tok = mk.step("v", [lambda e, kt=kt: e.memset(zc.t(kt)[:, 0:1], 0.0)
                                    for kt in range(4, 8)], deps=[tok])
            return tok

        stat_tok = []
        for ch in range(NCH):
            tok_z = cast_chunk(ch, dep_src)
            tok_sq = mk.step("v", [lambda e, kt=kt: e.tensor_mul(
                sq.t(kt), zc.t(kt), zc.t(kt)) for kt in range(DT)], deps=[tok_z])
            fns = []
            for kt in range(DT):
                fns.append(lambda e, kt=kt: e.matmul(
                    self.bankap(6, rows=1, cols=CH), self.ones_col, zc.t(kt),
                    start=(kt == 0), stop=(kt == DT - 1)))
            for kt in range(DT):
                fns.append(lambda e, kt=kt: e.matmul(
                    self.bankap(7, rows=1, cols=CH), self.ones_col, sq.t(kt),
                    start=(kt == 0), stop=(kt == DT - 1)))
            tokm = mk.step("t", fns, deps=[tok_z, tok_sq, self.tok_cons], banks=[6, 7])
            toke = mk.step("s", [
                lambda e: e.copy(self.stat_rows[0:1, 0:CH], self.bankap(6, rows=1, cols=CH)),
                lambda e: e.copy(self.stat_rows[0:1, CH:2 * CH], self.bankap(7, rows=1, cols=CH)),
            ], deps=[tokm], banks=[6, 7])
            if ch == 1:
                self.mk.dump("statrows1", self.stat_rows, deps=[toke])
            fns = []
            for j in range(4):
                c = 4 * ch + j
                fns.append(lambda e, c=c, j=j: e.matmul(
                    PS[:, 512 * 5 + c:512 * 5 + c + 1],
                    self.stat_rows[0:1, 128 * j:128 * (j + 1)], self.one1f,
                    start=True, stop=True))
                fns.append(lambda e, c=c, j=j: e.matmul(
                    PS[:, 512 * 5 + 16 + c:512 * 5 + 16 + c + 1],
                    self.stat_rows[0:1, CH + 128 * j:CH + 128 * (j + 1)], self.one1f,
                    start=True, stop=True))
            stat_tok.append(mk.step("t", fns, deps=[toke], banks=[5]))

        tm = self.ln_tm
        tok_v = mk.step("v", [
            lambda e: e.tensor_copy(tm[:, 0:32], PS[:, 512 * 5:512 * 5 + 32]),
            lambda e: e.tensor_scalar_mul(tm[:, 32:48], tm[:, 0:16], 1.0 / D),
            lambda e: e.tensor_scalar_mul(tm[:, 48:64], tm[:, 16:32], 1.0 / D),
            lambda e: e.tensor_mul(tm[:, 64:80], tm[:, 32:48], tm[:, 32:48]),
            lambda e: e.tensor_sub(tm[:, 64:80], tm[:, 48:64], tm[:, 64:80]),
            lambda e: e.tensor_scalar_add(tm[:, 64:80], tm[:, 64:80], LN_EPS),
            lambda e: e.reciprocal(tm[:, 64:80], tm[:, 64:80]),
        ], deps=stat_tok, banks=[5])
        tok_s = mk.step("s", [
            lambda e: e.activation(tm[:, 80:96], tm[:, 64:80], AF.Sqrt),
        ], deps=[tok_v])
        tok_v2 = mk.step("v", [
            lambda e: e.tensor_scalar_mul(tm[:, 96:112], tm[:, 32:48], -1.0),
            lambda e: e.tensor_mul(tm[:, 96:112], tm[:, 96:112], tm[:, 80:96]),
        ], deps=[tok_s])
        self.mk.dump("lntm", tm, deps=[tok_v2])
        tok_tr = mk.step("t", [
            lambda e: e.matmul(PS[0:16, 512 * 5:512 * 5 + 128],
                               tm[:, 80:96], self.identf, start=True, stop=True),
            lambda e: e.matmul(PS[0:16, 512 * 5 + 128:512 * 5 + 256],
                               tm[:, 96:112], self.identf, start=True, stop=True),
        ], deps=[tok_v2, self.tok_cons], banks=[5])
        tok_r = mk.step("s", [
            lambda e: e.copy(self.ln_rows[:, 0:256], PS[0:16, 512 * 5:512 * 5 + 256]),
        ], deps=[tok_tr], banks=[5])

        self.mk.dump("lnrows", self.ln_rows, deps=[tok_r])
        out_toks = []
        for ch in range(NCH):
            fns = []
            for j in range(4):
                c = 4 * ch + j
                fns.append(lambda e, c=c, j=j: e.matmul(
                    PS[:, 512 * 5 + 128 * j:512 * 5 + 128 * (j + 1)],
                    self.lnsel[:, 128 * c:128 * (c + 1)],
                    self.ln_rows[:, 0:128], start=True, stop=True))
                fns.append(lambda e, c=c, j=j: e.matmul(
                    PS[:, 512 * 6 + 128 * j:512 * 6 + 128 * (j + 1)],
                    self.lnsel[:, 128 * c:128 * (c + 1)],
                    self.ln_rows[:, 128:256], start=True, stop=True))
            tok_b = mk.step("t", fns, deps=[tok_r], banks=[5, 6])
            if out_bf is None:
                for kt in range(DT):
                    tok = mk.step("v", [
                        lambda e, kt=kt, ch=ch: e.tensor_mul(
                            src.t(kt)[:, ch * CH:(ch + 1) * CH],
                            src.t(kt)[:, ch * CH:(ch + 1) * CH],
                            PS[:, 512 * 5:512 * 5 + CH]),
                        lambda e, kt=kt, ch=ch: e.tensor_add(
                            src.t(kt)[:, ch * CH:(ch + 1) * CH],
                            src.t(kt)[:, ch * CH:(ch + 1) * CH],
                            PS[:, 512 * 6:512 * 6 + CH]),
                    ], deps=[tok_b], banks=[5, 6])
                    out_toks.append(tok)
            else:
                tok_z = cast_chunk(ch, dep_src + [tok_b])
                tok_prev_s = None
                for kt in range(DT):
                    vdeps = [tok_z, tok_b]
                    if tok_prev_s is not None:
                        vdeps.append(tok_prev_s)
                    tokv = mk.step("v", [
                        lambda e, kt=kt: e.tensor_mul(
                            self.ln_tmp, zc.t(kt), PS[:, 512 * 5:512 * 5 + CH]),
                        lambda e, kt=kt: e.tensor_add(
                            self.ln_tmp, self.ln_tmp, PS[:, 512 * 6:512 * 6 + CH]),
                    ], deps=vdeps, banks=[5, 6])
                    tok = mk.step("s", [
                        lambda e, kt=kt, ch=ch: e.activation(
                            out_bf.t(kt)[:, ch * CH:(ch + 1) * CH], self.ln_tmp,
                            AF.Identity, bias=beta[:, kt:kt + 1], scale=gamma[:, kt:kt + 1]),
                    ], deps=[tokv])
                    tok_prev_s = tok
                    out_toks.append(tok)
        return out_toks

    def emit_stageA(self):
        mk = self.mk
        A = self.arena
        A.mark()
        xb = A.alloc(DT, T, BF16)
        qf = [A.alloc(1, T, BF16) for _ in range(2)]
        kf = [A.alloc(1, T, BF16) for _ in range(2)]
        qsl = [A.alloc(1, 8 * 128, BF16) for _ in range(2)]
        ksl = [A.alloc(1, 8 * 128, BF16) for _ in range(2)]
        wbig = A.alloc(1, 8 * 1024, BF16)
        attc = [A.alloc(DT, CH, BF16) for _ in range(2)]
        v_aug = A.at(0, TT, 16 * 65, BF16)
        vs_aug = A.at(33280, 4, 16 * 65, BF16)
        es = [A.at(41600 + i * 2048, 1, 1024, BF16) for i in range(2)]
        el = [A.at(45696 + i * 1536, 1, 768, BF16) for i in range(2)]
        att_st = [A.at(48768 + i * 4096, 1, T, BF16) for i in range(2)]
        assert 48768 + 2 * 4096 <= DT * T * 4

        tok_vw = mk.dma("g", wbig.t(0), self.h_vw)

        tok_xb = [mk.step("v", [lambda e, kt=kt: e.tensor_copy(xb.t(kt), self.x.t(kt))],
                          deps=[self.tok_x]) for kt in range(DT)]
        tok_xb_all = tok_xb[-1]

        def ones_ap(buf, tt):
            return buf.t(tt).rearrange("p (h c) -> p h c", h=16)[:, :, 64:65]
        tok_ones = mk.step("v", [lambda e, tt=tt, b=b: e.memset(ones_ap(b, tt), 1.0)
                                 for b in (v_aug, vs_aug) for tt in range(b.tiles)],
                           deps=[tok_xb_all])

        vtoks = []
        for strided in (0, 1):
            buf = (v_aug, vs_aug)[strided]
            for tt in range(buf.tiles):
                for dc in range(2):
                    b = self.nb()
                    fns = []
                    for kt in range(DT):
                        if strided:
                            lhs = xb.t(kt)[:, 512 * tt:512 * (tt + 1):4]
                        else:
                            lhs = xb.t(kt)[:, 128 * tt:128 * (tt + 1)]
                        fns.append(lambda e, kt=kt, dc=dc, b=b, lhs=lhs: e.matmul(
                            self.bankap(b), lhs,
                            wbig.t(0)[:, 1024 * kt + 512 * dc:1024 * kt + 512 * (dc + 1)],
                            start=(kt == 0), stop=(kt == DT - 1)))
                    tokm = mk.step("t", fns, deps=[tok_xb_all, tok_vw], banks=[b])
                    dst = buf.t(tt).rearrange("p (h c) -> p h c", h=16)[:, 8 * dc:8 * (dc + 1), 0:64]
                    src = self.bankap(b).rearrange("p (h c) -> p h c", h=8)
                    vtoks.append(mk.step("v", [lambda e, dst=dst, src=src: e.tensor_copy(dst, src)],
                                         deps=[tokm, tok_ones], banks=[b]))
        tok_v_done = vtoks[-1]
        self.mk.dump("vaug0", v_aug.t(0), deps=[tok_v_done])
        self.mk.dump("vsaug0", vs_aug.t(0), deps=[tok_v_done])

        unit = [0]
        tok_pair = {}
        attn_last = {}
        av_last = {}
        attst_tok = {}
        tok_den = []
        den_dma = {}
        for mt in range(8):
            pb = mt % 2
            dep_prev = [tok_pair[mt - 2]] if mt - 2 in tok_pair else []
            tok_lq = mk.dma("g", qsl[pb].t(0).rearrange("p (k m) -> p k m", k=8),
                            self.h_qw.rearrange("p (k m) -> p k m", k=8)[:, :, 128 * mt:128 * (mt + 1)],
                            deps=dep_prev)
            tok_lk = mk.dma("g", ksl[pb].t(0).rearrange("p (k m) -> p k m", k=8),
                            self.h_kw.rearrange("p (k m) -> p k m", k=8)[:, :, 128 * mt:128 * (mt + 1)],
                            deps=dep_prev)
            ptoks = []
            for ch in range(NCH):
                for wsl, dstb, scaled in ((qsl[pb], qf[pb], True), (ksl[pb], kf[pb], False)):
                    b = self.nb()
                    fns = []
                    for kt in range(DT):
                        fns.append(lambda e, kt=kt, ch=ch, b=b, wsl=wsl: e.matmul(
                            self.bankap(b), wsl.t(0)[:, 128 * kt:128 * (kt + 1)],
                            xb.t(kt)[:, CH * ch:CH * (ch + 1)],
                            start=(kt == 0), stop=(kt == DT - 1)))
                    tokm = mk.step("t", fns, deps=[tok_lq, tok_lk, tok_xb_all], banks=[b])
                    gdeps = [attn_last[pb]] if pb in attn_last else []
                    if scaled:
                        tok = mk.step("s", [lambda e, ch=ch, b=b, dstb=dstb: e.activation(
                            dstb.t(0)[:, CH * ch:CH * (ch + 1)], self.bankap(b),
                            AF.Copy, scale=self.c_ap[:, 0:1])],
                            deps=[tokm, self.tok_c] + gdeps, banks=[b])
                    else:
                        tok = mk.step("v", [lambda e, ch=ch, b=b, dstb=dstb: e.tensor_copy(
                            dstb.t(0)[:, CH * ch:CH * (ch + 1)], self.bankap(b))],
                            deps=[tokm] + gdeps, banks=[b])
                    ptoks.append(tok)
            tok_pair[mt] = ptoks[-1]
            if mt == 0:
                self.mk.dump("qf0", qf[0].t(0), deps=ptoks)
                self.mk.dump("kf0", kf[0].t(0), deps=ptoks)

            prev_spill = attst_tok.get(pb)
            for h2 in range(2):
                h = 2 * mt + h2
                r0 = 64 * h2
                for qc in range(8):
                    u = unit[0]
                    unit[0] += 1
                    p = u % 2
                    sreg = 512 * (2 * p)
                    lreg = 2048 + 768 * p
                    lbanks = [4, 5] if p == 0 else [5, 6]
                    ch512 = qc // 2
                    avbank = 7
                    avoff = 256 * (qc % 2)
                    q0 = 256 * qc
                    fns = []
                    for kst in range(4):
                        fns.append(lambda e, kst=kst, pb=pb, r0=r0, q0=q0, sreg=sreg: e.matmul(
                            self.PS[:, sreg + 256 * kst: sreg + 256 * (kst + 1)],
                            kf[pb].t(0)[r0:r0 + 64, 512 * kst:512 * (kst + 1):4],
                            qf[pb].t(0)[r0:r0 + 64, q0:q0 + 256],
                            start=True, stop=True))
                    tok_sc = mk.step("t", fns, deps=ptoks[-2:], banks=[2 * p, 2 * p + 1])
                    fns = []
                    for qi in range(2):
                        qt = 2 * qc + qi
                        for bl in range(3):
                            ktile = min(max(qt - 1 + bl, 0), TT - 1)
                            fns.append(lambda e, pb=pb, r0=r0, qt=qt, qi=qi, bl=bl,
                                       ktile=ktile, lreg=lreg: e.matmul(
                                self.PS[:, lreg + 384 * qi + 128 * bl:
                                        lreg + 384 * qi + 128 * (bl + 1)],
                                kf[pb].t(0)[r0:r0 + 64, 128 * ktile:128 * (ktile + 1)],
                                qf[pb].t(0)[r0:r0 + 64, 128 * qt:128 * (qt + 1)],
                                start=True, stop=True))
                    tok_lc = mk.step("t", fns, deps=ptoks[-2:], banks=lbanks)
                    mask = (self.maskA_first if qc == 0 else
                            (self.maskA_last if qc == 7 else self.maskA_cat))
                    tok_mk = mk.step("v", [lambda e, lreg=lreg, mask=mask: e.tensor_add(
                        self.PS[:, lreg:lreg + 768],
                        self.PS[:, lreg:lreg + 768], mask)],
                        deps=[tok_lc, self.tok_cons], banks=lbanks)
                    edeps = [tok_sc, tok_mk]
                    if p in av_last:
                        edeps.append(av_last[p])
                    tok_e = mk.step("s", [
                        lambda e, sreg=sreg, p=p: e.activation(
                            es[p].t(0), self.PS[:, sreg:sreg + 1024], AF.Exp),
                        lambda e, lreg=lreg, p=p: e.activation(
                            el[p].t(0), self.PS[:, lreg:lreg + 768], AF.Exp),
                    ], deps=edeps, banks=[2 * p, 2 * p + 1] + lbanks)
                    fns = []
                    for kst in range(4):
                        fns.append(lambda e, kst=kst, h=h, p=p, avbank=avbank,
                                   avoff=avoff: e.matmul(
                            self.PS[0:65, 512 * avbank + avoff:512 * avbank + avoff + 256],
                            vs_aug.t(kst)[:, 65 * h:65 * h + 65],
                            es[p].t(0)[:, 256 * kst:256 * (kst + 1)],
                            start=(kst == 0), stop=False))
                    for qi in range(2):
                        qt = 2 * qc + qi
                        for bl in range(3):
                            ktile = min(max(qt - 1 + bl, 0), TT - 1)
                            fns.append(lambda e, qt=qt, qi=qi, bl=bl, ktile=ktile, h=h,
                                       p=p, avbank=avbank, avoff=avoff: e.matmul(
                                self.PS[0:65, 512 * avbank + avoff + 128 * qi:
                                        512 * avbank + avoff + 128 * (qi + 1)],
                                v_aug.t(ktile)[:, 65 * h:65 * h + 65],
                                el[p].t(0)[:, 384 * qi + 128 * bl:384 * qi + 128 * (bl + 1)],
                                start=False, stop=(qi == 1 and bl == 2)))
                    if h == 0 and qc == 0:
                        self.mk.dump("es00", es[p].t(0), deps=[tok_e])
                        self.mk.dump("el00", el[p].t(0), deps=[tok_e])
                    tok_av = mk.step("t", fns, deps=[tok_e, tok_v_done], banks=[avbank])
                    attn_last[pb] = tok_lc
                    av_last[p] = tok_av
                    if qc % 2 == 1:
                        evdeps = [tok_av]
                        if prev_spill is not None:
                            evdeps.append(prev_spill)
                        tok_ev = mk.step("v", [lambda e, h2=h2, ch512=ch512, pb=pb,
                                               avbank=avbank: e.tensor_copy(
                            att_st[pb].t(0)[64 * h2:64 * h2 + 64, CH * ch512:CH * (ch512 + 1)],
                            self.PS[0:64, 512 * avbank:512 * (avbank + 1)])],
                            deps=evdeps, banks=[avbank])
                        dsb = (h * NCH + ch512) % 2
                        ddeps = [tok_av]
                        if dsb in den_dma:
                            ddeps.append(den_dma[dsb])
                        tok_dn = mk.step("s", [lambda e, dsb=dsb, avbank=avbank: e.copy(
                            self.denstage[dsb],
                            self.PS[64:65, 512 * avbank:512 * (avbank + 1)])],
                            deps=ddeps, banks=[avbank])
                        den_dma[dsb] = mk.dma(
                            "y", self.den16[h:h + 1, CH * ch512:CH * (ch512 + 1)],
                            self.denstage[dsb], deps=[tok_dn])
                        tok_den.append(den_dma[dsb])
                        av_last[p] = tok_ev
            attst_tok[pb] = mk.dma(
                "y", self.att_sp.ap()[:, mt * T * 1:(mt + 1) * T], att_st[pb].t(0),
                deps=[av_last[0], av_last[1]], store=True)
        tok_att_spill = [attst_tok[0], attst_tok[1]]

        tok_r = mk.step("v", [
            lambda e: e.reciprocal(self.r16f, self.den16),
        ], deps=tok_den)
        self.mk.dump("den16", self.den16, deps=[tok_r])
        self.mk.dump("r16f", self.r16f, deps=[tok_r])

        tok_ow = mk.dma("g", wbig.t(0), self.h_ow, deps=vtoks[-1:])
        tok_xrl = mk.dma("y", self.x.full(), self.x_in.ap(),
                         deps=[av_last[0], av_last[1], tok_ones] + vtoks[-1:] + tok_att_spill)
        tok_res = []
        attrl_tok = {}
        for ch in range(NCH):
            cb = ch % 2
            rdeps = tok_att_spill + ([attrl_tok[cb]] if cb in attrl_tok else [])
            tok_arl = mk.dma("y", attc[cb].full().rearrange("p (k c) -> p k c", k=DT),
                             self.att_sp.ap().rearrange("p (k t) -> p k t", k=DT)
                             [:, :, CH * ch:CH * (ch + 1)], deps=rdeps)
            ntoks = []
            for h in range(NH_A):
                b = 4 + (h % 2)
                tok_b = mk.step("t", [
                    lambda e, h=h, ch=ch, b=b: e.matmul(
                        self.PS[0:64, 512 * b:512 * b + CH],
                        self.selk16[:, 64 * h:64 * (h + 1)],
                        self.r16f[:, CH * ch:CH * (ch + 1)], start=True, stop=True),
                ], deps=[tok_r, tok_arl], banks=[b])
                ntoks.append(mk.step("v", [lambda e, h=h, cb=cb, b=b: e.tensor_mul(
                    attc[cb].t(h // 2)[64 * (h % 2):64 * (h % 2) + 64, :],
                    attc[cb].t(h // 2)[64 * (h % 2):64 * (h % 2) + 64, :],
                    self.PS[0:64, 512 * b:512 * b + CH])],
                    deps=[tok_b, tok_arl], banks=[b]))
            for mt in range(DT):
                b = self.nb(0, 4)
                fns = []
                for kt in range(DT):
                    fns.append(lambda e, kt=kt, mt=mt, cb=cb, b=b: e.matmul(
                        self.bankap(b), wbig.t(0)[:, 1024 * kt + 128 * mt:1024 * kt + 128 * (mt + 1)],
                        attc[cb].t(kt), start=(kt == 0), stop=(kt == DT - 1)))
                tokm = mk.step("t", fns, deps=[tok_ow] + ntoks, banks=[b])
                tokv = mk.step("v", [lambda e, mt=mt, ch=ch, b=b: e.tensor_add(
                    self.x.t(mt)[:, CH * ch:CH * (ch + 1)],
                    self.x.t(mt)[:, CH * ch:(ch + 1) * CH], self.bankap(b))],
                    deps=[tokm, tok_xrl], banks=[b])
                toks = mk.step("s", [lambda e, mt=mt, ch=ch: e.activation(
                    self.x.t(mt)[:, CH * ch:(ch + 1) * CH],
                    self.x.t(mt)[:, CH * ch:(ch + 1) * CH],
                    AF.Identity, bias=self.ob_t[:, mt:mt + 1])],
                    deps=[tokv])
                tok_res.append(toks)
            if ch == 0:
                self.mk.dump("attn0", attc[0].full(), deps=ntoks)
            attrl_tok[cb] = tokm

        self.mk.dump("zres", self.x.full(), deps=tok_res)
        A.reset()
        A.mark()
        zc = A.alloc(DT, CH, BF16)
        sq = A.alloc(DT, CH, BF16)
        ln_toks = self.emit_ln(self.x, tok_res, zc, sq)
        A.reset()
        self.tok_last = ln_toks
        self.tok_spill_b = mk.dma("y", self.spill_b.ap(), self.x.full(),
                                  deps=ln_toks, store=True)
        self.dumpb("xA", self.x, ln_toks)

    def emit_stageB_layer(self, l):
        mk = self.mk
        A = self.arena
        PS = self.PS
        A.mark()
        h_sh = A.at(0, DT, T, BF16)
        tabs = A.at(32768, 2, T, BF16)
        EB = [A.at(40960 + i * 1280, 1, 640, BF16) for i in range(2)]
        v_augB = A.at(43520, TT, 8 * 65, BF16)
        qraw = A.alloc(4, T, BF16)
        kraw = A.alloc(4, T, BF16)
        q8 = A.alloc(4, T, BF16)
        attB = A.alloc(4, T, BF16)
        wo_sb = A.alloc(1, 4 * 1024, BF16)
        wv_sb = A.alloc(1, 8 * 512, BF16)
        wsl = [A.alloc(1, 8 * 128, BF16) for _ in range(2)]
        sqb = [A.alloc(1, T, BF16)]
        tmpb = [A.alloc(1, CH, BF16) for _ in range(2)]
        kr = qraw

        wqkv_r = self.h_wqkv[l].rearrange("p (k m) -> p k m", k=DT)

        tok_spx = mk.dma("y", self.spill_x.ap(), self.x.full(),
                         deps=self.tok_last, store=True)
        tok_h = []
        for kt in range(DT):
            if kt < 4:
                tok_h.append(mk.step("s", [lambda e, kt=kt: e.copy(
                    h_sh.t(kt), self.x.t(kt))], deps=self.tok_last + [tok_spx]))
            else:
                tok_h.append(mk.step("s", [lambda e, kt=kt: e.copy(
                    h_sh.t(kt)[:, 1:T], self.x.t(kt)[:, 0:T - 1])],
                    deps=self.tok_last + [tok_spx]))
        tok_h.append(mk.step("v", [lambda e, kt=kt: e.memset(h_sh.t(kt)[:, 0:1], 0.0)
                                   for kt in range(4, 8)], deps=tok_h))
        tok_h_all = tok_h[-1]
        if l == 0:
            self.mk.dump("hB0", h_sh.full(), deps=[tok_h_all])

        vdep = mk.dma("g", wv_sb.t(0).rearrange("p (k m) -> p k m", k=DT),
                      wqkv_r[:, :, 1024:1536], deps=self.tok_last)
        ptoks = []
        proj_done = {}
        for which in range(2):
            dst = (qraw, kraw)[which]
            for mt in range(4):
                pb = (which * 4 + mt) % 2
                ldeps = [proj_done[pb]] if pb in proj_done else list(self.tok_last)
                tok_l = mk.dma("g", wsl[pb].t(0).rearrange("p (k m) -> p k m", k=DT),
                               wqkv_r[:, :, 512 * which + 128 * mt:
                                      512 * which + 128 * (mt + 1)], deps=ldeps)
                if l == 0 and which == 0 and mt == 0:
                    self.mk.dump("wsl0", wsl[0].t(0), deps=[tok_l])
                    self.stop_check("wsl")
                for ch in range(NCH):
                    b = self.nb(0, 4)
                    fns = []
                    for kt in range(DT):
                        fns.append(lambda e, kt=kt, ch=ch, b=b, pb=pb: e.matmul(
                            self.bankap(b), wsl[pb].t(0)[:, 128 * kt:128 * (kt + 1)],
                            h_sh.t(kt)[:, CH * ch:CH * (ch + 1)],
                            start=(kt == 0), stop=(kt == DT - 1)))
                    tokm = mk.step("t", fns, deps=[tok_l, tok_h_all], banks=[b])
                    ptoks.append(mk.step("v", [lambda e, ch=ch, b=b, dst=dst, mt=mt:
                                               e.tensor_copy(
                        dst.t(mt)[:, CH * ch:CH * (ch + 1)], self.bankap(b))],
                        deps=[tokm], banks=[b]))
                proj_done[pb] = tokm
        tok_qk = ptoks[-1]
        if l == 0:
            self.mk.dump("qraw0", qraw.t(0), deps=[tok_qk])
            self.mk.dump("kraw0", kraw.t(0), deps=[tok_qk])
            self.mk.dump("wsl0b", wsl[0].t(0), deps=[tok_qk])
            self.stop_check("qraw")
        tok_vones = mk.step("v", [lambda e, tt=tt: e.memset(
            v_augB.t(tt).rearrange("p (h c) -> p h c", h=8)[:, :, 64:65], 1.0)
            for tt in range(TT)], deps=self.tok_last + [tok_spx])
        vtoks = []
        for tt in range(TT):
            b = self.nb(0, 4)
            fns = []
            for kt in range(DT):
                fns.append(lambda e, kt=kt, tt=tt, b=b: e.matmul(
                    self.bankap(b), h_sh.t(kt)[:, 128 * tt:128 * (tt + 1)],
                    wv_sb.t(0)[:, 512 * kt:512 * (kt + 1)],
                    start=(kt == 0), stop=(kt == DT - 1)))
            tokm = mk.step("t", fns, deps=[vdep, tok_h_all], banks=[b])
            dst = v_augB.t(tt).rearrange("p (h c) -> p h c", h=8)[:, :, 0:64]
            src = self.bankap(b).rearrange("p (h c) -> p h c", h=8)
            vtoks.append(mk.step("v", [lambda e, dst=dst, src=src: e.tensor_copy(dst, src)],
                                 deps=[tokm, tok_vones], banks=[b]))
        tok_v_done = vtoks[-1]

        mm_tok = {}
        sq_tok = {}
        for j in range(8):
            src = (qraw, kraw)[j // 4]
            tl = j % 4
            sdeps = [tok_qk]
            if j - 1 in mm_tok:
                sdeps.append(mm_tok[j - 1])
            sq_tok[j] = mk.step("s", [lambda e, src=src, tl=tl, j=j: e.activation(
                sqb[0].t(0), src.t(tl), AF.Square)], deps=sdeps)
            fns = []
            for ch in range(NCH):
                fns.append(lambda e, j=j, ch=ch: e.matmul(
                    PS[0:16, 2048 + CH * ch:2048 + CH * (ch + 1)],
                    self.sumsq_lhsT[:, 16 * j:16 * (j + 1)],
                    sqb[0].t(0)[:, CH * ch:CH * (ch + 1)],
                    start=(j == 0), stop=(j == 7)))
            mm_tok[j] = mk.step("t", fns, deps=[sq_tok[j]], banks=[4, 5, 6, 7])
        tok_rb = mk.step("v", [
            lambda e: e.tensor_scalar_max(self.r16f, PS[0:16, 2048:4096], 1e-24),
            lambda e: e.reciprocal(self.r16f, self.r16f),
        ], deps=[mm_tok[7]], banks=[4, 5, 6, 7])
        tok_rb = mk.step("s", [
            lambda e: e.activation(self.r16f, self.r16f, AF.Sqrt),
        ], deps=[tok_rb])
        if l == 0:
            self.mk.dump("rbB", self.r16f, deps=[tok_rb])
            self.stop_check("rb")
        hat_tok = {}
        for j in range(8):
            src = (qraw, kraw)[j // 4]
            tl = j % 4
            sc = (self.qs_t, self.ks_t)[j // 4][:, l:l + 1]
            for ch in range(NCH):
                b = self.nb(0, 4)
                tok_b = mk.step("t", [lambda e, j=j, ch=ch, b=b: e.matmul(
                    self.bankap(b), self.bcast_lhsT[:, 128 * j:128 * (j + 1)],
                    self.r16f[:, CH * ch:CH * (ch + 1)], start=True, stop=True)],
                    deps=[tok_rb], banks=[b])
                hat_tok[(j, ch)] = mk.step("v", [lambda e, src=src, tl=tl, ch=ch, b=b,
                                                 sc=sc: e.scalar_tensor_tensor(
                    src.t(tl)[:, CH * ch:CH * (ch + 1)],
                    src.t(tl)[:, CH * ch:CH * (ch + 1)], sc,
                    self.bankap(b), OP.mult, OP.mult)],
                    deps=[tok_b], banks=[b])
        tok_hat = list(hat_tok.values())[-1]

        def rotary(srcb, dstb, cnm, snm, deps):
            tok_tab = mk.dma("g", tabs.t(0), self.h_rot[cnm], deps=deps + [tok_spx])
            tok_tab2 = mk.dma("g", tabs.t(1), self.h_rot[snm], deps=deps + [tok_spx])
            outs = []
            for tl in range(4):
                for ch in range(NCH):
                    b = self.nb(0, 4)
                    tok_rot = mk.step("t", [lambda e, srcb=srcb, tl=tl, ch=ch, b=b: e.matmul(
                        self.bankap(b), self.rotT,
                        srcb.t(tl)[:, CH * ch:CH * (ch + 1)], start=True, stop=True)],
                        deps=[tok_tab, tok_tab2] + deps, banks=[b])
                    sl = (4 * tl + ch) % 2
                    t1 = mk.step("v", [lambda e, srcb=srcb, tl=tl, ch=ch, sl=sl: e.tensor_mul(
                        tmpb[sl].t(0),
                        srcb.t(tl)[:, CH * ch:CH * (ch + 1)],
                        tabs.t(0)[:, CH * ch:CH * (ch + 1)])],
                        deps=[tok_tab, tok_tab2] + deps)
                    t2 = mk.step("v", [lambda e, dstb=dstb, tl=tl, ch=ch, b=b: e.tensor_mul(
                        dstb.t(tl)[:, CH * ch:CH * (ch + 1)],
                        self.bankap(b),
                        tabs.t(1)[:, CH * ch:CH * (ch + 1)])],
                        deps=[tok_rot], banks=[b])
                    t3 = mk.step("v", [lambda e, dstb=dstb, tl=tl, ch=ch, sl=sl: e.tensor_add(
                        dstb.t(tl)[:, CH * ch:CH * (ch + 1)],
                        dstb.t(tl)[:, CH * ch:CH * (ch + 1)],
                        tmpb[sl].t(0))],
                        deps=[t1, t2])
                    outs.append(t3)
            return outs

        if l == 0:
            self.mk.dump("qhat0", qraw.t(0), deps=[tok_hat])
            self.stop_check("qhat")
        rq = rotary(qraw, q8, "CQ8", "SQ8", [tok_hat])
        rk = rotary(kraw, kr, "CK", "SK", [rq[-1]])
        tok_rot_all = rk[-1]

        if l == 0:
            self.mk.dump("q8r0", q8.t(0), deps=[tok_rot_all])
            self.mk.dump("kr0", kr.t(0), deps=[tok_rot_all])
            self.stop_check("rot")
        av_last = {}
        eb_last = {}
        ks_last = {}
        tok_den = []
        den_dma = {}
        attB_ev = []
        mstage = self.mrow
        u = 0
        for h in range(NH_B):
            tl, r0 = h // 2, 64 * (h % 2)
            for qt in range(TT):
                p = u % 2
                u += 1
                ksreg = (0, 1024)[p]
                ksbanks = ([0, 1], [2, 3])[p]
                dslot = 3072 + 128 * p
                mslot = 3328 + 128 * p
                tok_d = mk.step("t", [lambda e, tl=tl, r0=r0, qt=qt, dslot=dslot: e.matmul(
                    PS[:, dslot:dslot + 128],
                    q8.t(tl)[r0:r0 + 64, 128 * qt:128 * (qt + 1)],
                    kr.t(tl)[r0:r0 + 64, 128 * qt:128 * (qt + 1)],
                    start=True, stop=True)],
                    deps=[tok_rot_all], banks=[6])
                tok_m = mk.step("v", [lambda e, p=p, dslot=dslot: e.reduce_max(
                    self.m8[:, p:p + 1], PS[:, dslot:dslot + 128], axis=AX)],
                    deps=[tok_d], banks=[6])
                tok_mt = mk.step("t", [lambda e, p=p, mslot=mslot: e.matmul(
                    PS[0:1, mslot:mslot + 128], self.m8[:, p:p + 1], self.ident,
                    start=True, stop=True)], deps=[tok_m], banks=[6])
                msdeps = [tok_mt]
                if p in ks_last:
                    msdeps.append(ks_last[p])
                tok_ms = mk.step("s", [lambda e, p=p, mslot=mslot: e.copy(
                    mstage[0:1, 128 * p:128 * (p + 1)], PS[0:1, mslot:mslot + 128])],
                    deps=msdeps, banks=[6])

                nb_ = min(qt, 4) + 1
                kt0 = qt - (nb_ - 1)
                fns = []
                for bi in range(nb_):
                    kt = kt0 + bi
                    fns.append(lambda e, tl=tl, r0=r0, kt=kt, qt=qt, bi=bi,
                               ksreg=ksreg: e.matmul(
                        PS[:, ksreg + 128 * bi:ksreg + 128 * (bi + 1)],
                        kr.t(tl)[r0:r0 + 64, 128 * kt:128 * (kt + 1)],
                        q8.t(tl)[r0:r0 + 64, 128 * qt:128 * (qt + 1)],
                        start=True, stop=False))
                    fns.append(lambda e, bi=bi, ksreg=ksreg, p=p: e.matmul(
                        PS[:, ksreg + 128 * bi:ksreg + 128 * (bi + 1)],
                        self.negones_r128,
                        mstage[0:1, 128 * p:128 * (p + 1)],
                        start=False, stop=True))
                tok_ks = mk.step("t", fns, deps=[tok_ms, tok_rot_all], banks=ksbanks)
                ks_last[p] = tok_ks
                if qt >= 4:
                    mk_in0 = PS[:, ksreg:ksreg + 640].rearrange(
                        "p (a b) -> p a b", a=5)[:, 0:5:4, :]
                    mk_in1 = self.maskB_cat.rearrange("p (a b) -> p a b", a=2)
                    tok_mk = mk.step("v", [lambda e, mk_in0=mk_in0, mk_in1=mk_in1:
                                           e.tensor_tensor(mk_in0, mk_in0, mk_in1, OP.add)],
                                     deps=[tok_ks, self.tok_cons], banks=ksbanks)
                else:
                    lastoff = ksreg + 128 * (nb_ - 1)
                    tok_mk = mk.step("v", [lambda e, lastoff=lastoff: e.tensor_add(
                        PS[:, lastoff:lastoff + 128],
                        PS[:, lastoff:lastoff + 128],
                        self.maskB_cat[:, 128:256])],
                        deps=[tok_ks, self.tok_cons], banks=ksbanks)
                edeps = [tok_mk, tok_spx]
                if p in eb_last:
                    edeps.append(eb_last[p])
                tok_e = mk.step("s", [lambda e, ksreg=ksreg, nb_=nb_, p=p: e.activation(
                    EB[p].t(0)[:, 0:128 * nb_], PS[:, ksreg:ksreg + 128 * nb_], AF.Exp)],
                    deps=edeps, banks=ksbanks)
                ch512 = qt // 4
                avreg = 2048 + 512 * (ch512 % 2)
                avoff = 128 * (qt % 4)
                avfirst = (qt % 4 == 0)
                fns = []
                for bi in range(nb_):
                    kt = kt0 + bi
                    fns.append(lambda e, kt=kt, h=h, p=p, bi=bi, avreg=avreg,
                               avoff=avoff, nb_=nb_: e.matmul(
                        PS[0:65, avreg + avoff:avreg + avoff + 128],
                        v_augB.t(kt)[:, 65 * h:65 * h + 65],
                        EB[p].t(0)[:, 128 * bi:128 * (bi + 1)],
                        start=(bi == 0), stop=(bi == nb_ - 1)))
                tok_av = mk.step("t", fns, deps=[tok_e, tok_v_done],
                                 banks=[4 + (ch512 % 2)])
                eb_last[p] = tok_av
                if qt % 4 == 3:
                    tok_ev = mk.step("v", [lambda e, h=h, ch512=ch512, avreg=avreg:
                                           e.tensor_copy(
                        attB.t(h // 2)[64 * (h % 2):64 * (h % 2) + 64,
                                       CH * ch512:CH * (ch512 + 1)],
                        PS[0:64, avreg:avreg + 512])],
                        deps=[tok_av], banks=[4 + (ch512 % 2)])
                    dsb = (h + ch512) % 2
                    ddeps = [tok_av] + ([den_dma[dsb]] if dsb in den_dma else [])
                    tok_dn = mk.step("s", [lambda e, dsb=dsb, avreg=avreg: e.copy(
                        self.denstage[dsb], PS[64:65, avreg:avreg + 512])],
                        deps=ddeps, banks=[4 + (ch512 % 2)])
                    den_dma[dsb] = mk.dma(
                        "y", self.den16[h:h + 1, CH * ch512:CH * (ch512 + 1)],
                        self.denstage[dsb], deps=[tok_dn])
                    tok_den.append(den_dma[dsb])
                    attB_ev.append(tok_ev)
                    eb_last[p] = tok_ev

        tok_r = mk.step("v", [
            lambda e: e.reciprocal(self.r16f[0:8, :], self.den16[0:8, :]),
        ], deps=tok_den + attB_ev)
        ntoks = []
        for h in range(NH_B):
            for ch in range(NCH):
                b = 6 + (h % 2)
                tok_b = mk.step("t", [lambda e, h=h, ch=ch, b=b: e.matmul(
                    self.PS[0:64, 512 * b:512 * b + CH],
                    self.selk16[:, 64 * h:64 * (h + 1)],
                    self.r16f[:, CH * ch:CH * (ch + 1)], start=True, stop=True)],
                    deps=[tok_r], banks=[b])
                ntoks.append(mk.step("v", [lambda e, h=h, ch=ch, b=b: e.tensor_mul(
                    attB.t(h // 2)[64 * (h % 2):64 * (h % 2) + 64, CH * ch:CH * (ch + 1)],
                    attB.t(h // 2)[64 * (h % 2):64 * (h % 2) + 64, CH * ch:CH * (ch + 1)],
                    self.PS[0:64, 512 * b:512 * b + CH])],
                    deps=[tok_b], banks=[b]))

        if l == 0:
            self.mk.dump("denB", self.den16, deps=[tok_r])
            self.mk.dump("attB0", attB.full(), deps=ntoks)
            self.stop_check("att")
        tok_wo = mk.dma("g", wo_sb.t(0), self.h_wo[l], deps=self.tok_last)
        tok_xrl = mk.dma("y", self.x.full(), self.spill_x.ap(),
                         deps=[tok_spx, tok_rot_all, tok_v_done, tok_h_all] + attB_ev)
        tok_res = []
        for ch in range(NCH):
            for mt in range(DT):
                b = self.nb(0, 4)
                fns = []
                for kt in range(4):
                    fns.append(lambda e, kt=kt, mt=mt, ch=ch, b=b: e.matmul(
                        self.bankap(b),
                        wo_sb.t(0)[:, 1024 * kt + 128 * mt:1024 * kt + 128 * (mt + 1)],
                        attB.t(kt)[:, CH * ch:CH * (ch + 1)],
                        start=(kt == 0), stop=(kt == 3)))
                tokm = mk.step("t", fns, deps=[tok_wo] + ntoks, banks=[b])
                tok_res.append(mk.step("v", [lambda e, mt=mt, ch=ch, b=b: e.tensor_add(
                    self.x.t(mt)[:, CH * ch:CH * (ch + 1)],
                    self.x.t(mt)[:, CH * ch:CH * (ch + 1)], self.bankap(b))],
                    deps=[tokm, tok_xrl], banks=[b]))
        A.reset()

        if l == 0:
            self.mk.dump("xwo", self.x.full(), deps=tok_res)
            self.stop_check("xwo")
        A.mark()
        hln = A.alloc(DT, T, BF16)
        A.mark()
        zc = A.alloc(DT, CH, BF16)
        sq = A.alloc(DT, CH, BF16)
        ln_toks = self.emit_ln(self.x, tok_res, zc, sq, shift=True,
                               gamma=self.lng_t[:, 8 * l:8 * (l + 1)],
                               beta=self.lnb_t[:, 8 * l:8 * (l + 1)],
                               out_bf=hln)
        A.reset()

        if l == 0:
            self.mk.dump("hlnB", hln.full(), deps=ln_toks)
            self.stop_check("hln")
        phases = [(0, 896, 7, 128), (896, 896, 7, 128), (1792, HID - 1792, 8, 42)]
        w1a_sb = A.alloc(1, 8 * 1024, BF16)
        w1g_sb = A.alloc(1, 8 * 1024, BF16)
        w2_sb = A.alloc(1, 8 * 1024, BF16)
        prod = A.alloc(8, CH, BF16)
        gg = A.alloc(1, CH, BF16)

        tok_b2 = []
        for kt in range(DT):
            tok_b2.append(mk.step("s", [lambda e, kt=kt: e.activation(
                self.x.t(kt), self.x.t(kt), AF.Identity,
                bias=self.b2_t[:, 8 * l + kt:8 * l + kt + 1])],
                deps=ln_toks + tok_res))

        wload_prev = None
        w2b_prev = None
        ffn_ld0 = None
        w1a_r = self.h_w1a[l].rearrange("p (k m) -> p k m", k=DT)
        w1g_r = self.h_w1g[l].rearrange("p (k m) -> p k m", k=DT)
        ffn_out = []
        for pi, (off, ncols, ntl, lastrows) in enumerate(phases):
            toff = off // 128
            ldeps = [wload_prev] if wload_prev else [ln_toks[-1], tok_res[-1]]
            tok_wa = mk.dma("g", w1a_sb.t(0).rearrange("p (k m) -> p k m", k=DT)
                            [:, :, 0:ncols], w1a_r[:, :, off:off + ncols], deps=ldeps)
            tok_wg = mk.dma("g", w1g_sb.t(0).rearrange("p (k m) -> p k m", k=DT)
                            [:, :, 0:ncols], w1g_r[:, :, off:off + ncols], deps=ldeps)
            tok_w2 = mk.dma("g", w2_sb.t(0)[:, 0:ntl * 1024],
                            self.h_w2[l][:, 1024 * toff:1024 * (toff + ntl)], deps=ldeps)
            for ch in range(NCH):
                prod_toks = {}
                prod_prev = None
                for j in range(ntl):
                    rows = lastrows if j == ntl - 1 else 128
                    fns = []
                    for kt in range(DT):
                        fns.append(lambda e, kt=kt, j=j, ch=ch, rows=rows: e.matmul(
                            PS[0:rows, 2048:2048 + CH],
                            w1a_sb.t(0).rearrange("p (k m) -> p k m", k=DT)
                            [:, kt, 128 * j:128 * j + rows],
                            hln.t(kt)[:, CH * ch:CH * (ch + 1)],
                            start=(kt == 0), stop=(kt == DT - 1)))
                    for kt in range(DT):
                        fns.append(lambda e, kt=kt, j=j, ch=ch, rows=rows: e.matmul(
                            PS[0:rows, 2560:2560 + CH],
                            w1g_sb.t(0).rearrange("p (k m) -> p k m", k=DT)
                            [:, kt, 128 * j:128 * j + rows],
                            hln.t(kt)[:, CH * ch:CH * (ch + 1)],
                            start=(kt == 0), stop=(kt == DT - 1)))
                    tok_ag = mk.step("t", fns,
                                     deps=[tok_wa, tok_wg] + ln_toks, banks=[4, 5])
                    jg = toff + j
                    ggdeps = [tok_ag]
                    if prod_prev is not None:
                        ggdeps.append(prod_prev)
                    tok_gg = mk.step("s", [lambda e, rows=rows, jg=jg: e.activation(
                        gg.t(0)[0:rows, :], PS[0:rows, 2560:2560 + CH], AF.Gelu,
                        bias=self.b1g_t[0:rows, 22 * l + jg:22 * l + jg + 1])],
                        deps=ggdeps, banks=[5])
                    pdeps = [tok_gg]
                    if w2b_prev is not None:
                        pdeps.append(w2b_prev)
                    tok_p = mk.step("v", [lambda e, rows=rows, jg=jg, j=j:
                                          e.scalar_tensor_tensor(
                        prod.t(j)[0:rows, :], PS[0:rows, 2048:2048 + CH],
                        self.b1a_t[0:rows, 22 * l + jg:22 * l + jg + 1],
                        gg.t(0)[0:rows, :], OP.add, OP.mult)],
                        deps=pdeps, banks=[4])
                    prod_toks[j] = tok_p
                    prod_prev = tok_p
                    fns = []
                    for mt in range(4):
                        fns.append(lambda e, j=j, mt=mt, rows=rows, toff=toff: e.matmul(
                            PS[:, CH * mt:CH * (mt + 1)],
                            w2_sb.t(0)[0:rows, 1024 * j + 128 * mt:1024 * j + 128 * (mt + 1)],
                            prod.t(j)[0:rows, :],
                            start=(j == 0), stop=(j == ntl - 1)))
                    tok_w2a = mk.step("t", fns, deps=[tok_p, tok_w2], banks=[0, 1, 2, 3])
                fns = []
                for j in range(ntl):
                    rows = lastrows if j == ntl - 1 else 128
                    for mt in range(4, 8):
                        fns.append(lambda e, j=j, mt=mt, rows=rows: e.matmul(
                            PS[:, 2048 + CH * (mt - 4):2048 + CH * (mt - 3)],
                            w2_sb.t(0)[0:rows, 1024 * j + 128 * mt:1024 * j + 128 * (mt + 1)],
                            prod.t(j)[0:rows, :],
                            start=(j == 0), stop=(j == ntl - 1)))
                tok_w2b = mk.step("t", fns, deps=[prod_toks[ntl - 1], tok_w2],
                                  banks=[4, 5, 6, 7])
                w2b_prev = tok_w2b
                for mt in range(DT):
                    reg = CH * mt if mt < 4 else 2048 + CH * (mt - 4)
                    dep = tok_w2a if mt < 4 else tok_w2b
                    bk = mt if mt < 4 else 4 + (mt - 4)
                    ffn_out.append(mk.step("v", [lambda e, mt=mt, ch=ch, reg=reg: e.tensor_add(
                        self.x.t(mt)[:, CH * ch:CH * (ch + 1)],
                        self.x.t(mt)[:, CH * ch:CH * (ch + 1)],
                        PS[:, reg:reg + CH])],
                        deps=[dep] + tok_b2, banks=[bk]))
            wload_prev = ffn_out[-1]
        A.reset()

        self.tok_last = ffn_out
    def emit_final(self):
        mk = self.mk
        A = self.arena
        A.mark()
        xc = A.alloc(DT, T, F32)
        tok_rl = mk.dma("y", xc.full(), self.spill_b.ap(),
                        deps=[self.tok_spill_b] + self.tok_last)
        tok_add = []
        for kt in range(DT):
            tok_add.append(mk.step("v", [lambda e, kt=kt: e.tensor_add(
                self.x.t(kt), self.x.t(kt), xc.t(kt))],
                deps=[tok_rl] + self.tok_last))
        zc = A.alloc(DT, CH, BF16)
        sq = A.alloc(DT, CH, BF16)
        ln_toks = self.emit_ln(self.x, tok_add, zc, sq)
        A.reset()
        mk.dma("y", self.out_h.ap(), self.x.full(), deps=ln_toks, store=True)


def _prep_x(x):
    return np.ascontiguousarray(
        x.T.reshape(DT, 128, T).transpose(1, 0, 2).reshape(128, DT * T))


def _unprep_out(o):
    return o.reshape(128, DT, T).transpose(1, 0, 2).reshape(D, T).T


def build(inputs, debug=(), stages=("A", "B0", "B1", "F"), stop_after=None):
    p = Prog(inputs, debug=debug, stages=stages, stop_after=stop_after)
    p.emit()
    return p


def kernel(**inputs):
    inputs = {k: np.asarray(v, dtype=np.float32) for k, v in inputs.items()}
    p = build(inputs)
    x = inputs["L0_attn_x"]
    in_maps = []
    for c in range(8):
        m = dict(p.mk.inputs)
        m["x_fm"] = _prep_x(x[c])
        in_maps.append(m)
    res = run_bass_kernel_spmd(p.mk.nc, in_maps, core_ids=list(range(8)))
    out = np.stack([_unprep_out(res.results[c]["out_fm"]) for c in range(8)])
    return out.astype(np.float32)


# version 04
# speedup vs baseline: 1.3122x; 1.3122x over previous
from contextlib import ExitStack

import numpy as np
import ml_dtypes
import concourse.bass as bass
import concourse.mybir as mybir
from concourse.bass_utils import run_bass_kernel_spmd

F32 = mybir.dt.float32
BF16 = mybir.dt.bfloat16
AX = mybir.AxisListType.X
AF = mybir.ActivationFunctionType
OP = mybir.AluOpType
BF = ml_dtypes.bfloat16

T = 2048
D = 1024
DT = 8
TT = 16
NCH = 4
CH = 512
NEG = -1e30
HD = 64
NH_A = 16
STRIDE = 4
LOCAL = 8
NH_B = 8
INNER = 512
WINDOW = 512
HID = 2730
QK_SCALE = 8.0
LN_EPS = 1e-5
PH_OFF = (0, 1408)
PH_N = (1408, HID - 1408)
PH_TILES = (11, 11)
PH_LAST = (128, 42)


def cdiv(a, b):
    return (a + b - 1) // b


class StopEmission(Exception):
    pass


class MK:
    def __init__(self, debug=()):
        self.nc = bass.Bass("TRN2", target_bir_lowering=False, debug=False)
        self.stack = ExitStack()
        self.ops = {e: [] for e in ("t", "v", "s", "y", "g")}
        self.cnt = {e: 0 for e in ("t", "v", "s")}
        self.dma_cnt = {"LG": 0, "LY": 0, "ST": 0}
        self.sem = {}
        for name in ("T", "V", "S", "LG", "LY", "ST"):
            self.sem[name] = self.stack.enter_context(self.nc.semaphore(name))
        self.esem = {"t": "T", "v": "V", "s": "S"}
        self.waited = {e: {} for e in ("t", "v", "s", "y", "g")}
        self.bank = {b: {} for b in range(8)}
        self.debug = set(debug)
        self.dbg_names = []
        self.inputs = {}

    def _wait(self, eng, semname, val):
        if val <= 0:
            return
        if self.waited[eng].get(semname, -1) >= val:
            return
        self.waited[eng][semname] = val
        s = self.sem[semname]
        self.ops[eng].append(lambda e, s=s, v=val: e.wait_ge(s, v))

    def step(self, eng, fns, deps=(), banks=()):
        for semname, val in deps:
            if semname != self.esem[eng]:
                self._wait(eng, semname, val)
        for b in set(banks):
            for other, cval in list(self.bank[b].items()):
                if other != eng:
                    self._wait(eng, self.esem[other], cval)
        if not isinstance(fns, (list, tuple)):
            fns = [fns]
        self.cnt[eng] += 1
        my = self.cnt[eng]
        sem = self.sem[self.esem[eng]]
        guard = eng in ("v", "s")
        for i, fn in enumerate(fns):
            if guard:
                self.ops[eng].append(lambda e: e.nop())
            if i == len(fns) - 1:
                self.ops[eng].append(lambda e, fn=fn, sem=sem: fn(e).then_inc(sem, 1))
            else:
                self.ops[eng].append(fn)
        for b in set(banks):
            self.bank[b][eng] = my
        return (self.esem[eng], my)

    def dma(self, queue, out, in_, deps=(), store=False):
        semname = "ST" if store else {"g": "LG", "y": "LY"}[queue]
        for dsem, val in deps:
            self._wait(queue, dsem, val)
        self.dma_cnt[semname] += 16
        s = self.sem[semname]
        self.ops[queue].append(
            lambda e, out=out, in_=in_, s=s: e.dma_start(out=out, in_=in_).then_inc(s, 16))
        return (semname, self.dma_cnt[semname])

    def input(self, name, arr, sbuf=None, queue="g", deps=()):
        arr = np.ascontiguousarray(arr)
        dt = {np.dtype(np.float32): F32, np.dtype(BF): BF16}[arr.dtype]
        h = self.nc.dram_tensor(name, list(arr.shape), dt, kind="ExternalInput")
        self.inputs[name] = arr
        if sbuf is None:
            return h.ap()
        return self.dma(queue, sbuf, h.ap(), deps=deps)

    def dump(self, name, ap, deps=()):
        if name not in self.debug:
            return
        h = self.nc.dram_tensor("dbg_" + name, list(ap.shape), ap.dtype,
                                kind="ExternalOutput")
        self.dbg_names.append(name)
        self.dma("y", h.ap(), ap, deps=deps, store=True)

    def finish(self):
        st = self.dma_cnt["ST"]
        s = self.sem["ST"]
        self.ops["y"].append(lambda e, s=s, v=st: e.wait_ge(s, v))
        with self.nc.Block() as block:
            @block.tensor
            def _(t):
                for f in self.ops["t"]:
                    f(t)

            @block.vector
            def _(v):
                for f in self.ops["v"]:
                    f(v)

            @block.scalar
            def _(s_):
                for f in self.ops["s"]:
                    f(s_)

            @block.sync
            def _(y):
                for f in self.ops["y"]:
                    f(y)

            @block.gpsimd
            def _(g):
                for f in self.ops["g"]:
                    f(g)


class Buf:
    def __init__(self, ap_fn, tiles, cols, dtype):
        self.ap_fn = ap_fn
        self.tiles = tiles
        self.cols = cols
        self.dtype = dtype

    def t(self, i):
        return self.ap_fn(i)

    def full(self):
        return self.ap_fn(0, span=self.tiles)


class Arena:
    def __init__(self, nc, name, nbytes):
        self.h = nc.alloc_sbuf_tensor(name, [128, nbytes], mybir.dt.uint8)
        self.ap = self.h.ap()
        self.nbytes = nbytes
        self.off = 0
        self.marks = []

    def _mk(self, base, tiles, cols, dtype):
        esz = 2 if dtype == BF16 else 4
        per = cols * esz

        def ap_fn(i, span=1, base=base, per=per, dtype=dtype):
            return self.ap[:, base + i * per: base + (i + span) * per].bitcast(dtype)
        return Buf(ap_fn, tiles, cols, dtype)

    def alloc(self, tiles, cols, dtype):
        esz = 2 if dtype == BF16 else 4
        self.off = (self.off + 3) & ~3
        base = self.off
        self.off += tiles * cols * esz
        assert self.off <= self.nbytes, f"arena overflow {self.off} > {self.nbytes}"
        return self._mk(base, tiles, cols, dtype)

    def at(self, byte_off, tiles, cols, dtype):
        esz = 2 if dtype == BF16 else 4
        assert byte_off % 4 == 0
        assert byte_off + tiles * cols * esz <= self.nbytes
        return self._mk(byte_off, tiles, cols, dtype)

    def mark(self):
        self.marks.append(self.off)

    def reset(self):
        self.off = self.marks.pop()


def host_consts():
    c = {}
    c["ident"] = np.eye(128, dtype=np.float32).astype(BF)
    c["identf"] = np.eye(128, dtype=np.float32)
    il = np.arange(128)[None, :]
    jl = np.arange(128)[:, None]
    ok = (jl % STRIDE != 0)
    m_lo = np.where((np.abs(il - jl + 128) <= LOCAL) & ok, 0.0, NEG)
    m_mid = np.where((np.abs(il - jl) <= LOCAL) & ok, 0.0, NEG)
    m_hi2 = np.where((np.abs(il - jl - 128) <= LOCAL) & ok, 0.0, NEG)
    full = np.full((128, 128), NEG)
    norm3 = np.concatenate([m_lo, m_mid, m_hi2], 1)
    first3 = np.concatenate([full, m_mid, m_hi2], 1)
    last3 = np.concatenate([m_lo, m_mid, full], 1)
    c["maskA_cat"] = np.concatenate([norm3, norm3], 1).astype(BF)
    c["maskA_first"] = np.concatenate([first3, norm3], 1).astype(BF)
    c["maskA_last"] = np.concatenate([norm3, last3], 1).astype(BF)
    mB_lo = np.where(jl >= il, 0.0, NEG)
    mB_hi = np.where(jl <= il, 0.0, NEG)
    c["maskB_cat"] = np.concatenate([mB_lo, mB_hi], 1).astype(BF)
    inv_freq = 1.0 / 10000.0 ** (2.0 * np.arange(32, dtype=np.float64) / HD)
    freq = inv_freq[np.concatenate([np.arange(32), np.arange(32)])]
    t = np.arange(T, dtype=np.float64)
    theta = freq[:, None] * t[None, :]
    base = (2.0 * np.arange(32, dtype=np.float64) + 0.4 * HD) / (1.4 * HD)
    base = base[np.concatenate([np.arange(32), np.arange(32)])]
    power = (t - T // 2) / float(WINDOW // 2)
    scale = base[:, None] ** power[None, :]
    cos, sin = np.cos(theta), np.sin(theta)
    two = lambda a: np.concatenate([a, a], 0)
    c["CQ8"] = two(QK_SCALE * cos * scale).astype(BF)
    c["SQ8"] = two(QK_SCALE * sin * scale).astype(BF)
    c["CK"] = two(cos / scale).astype(BF)
    c["SK"] = two(sin / scale).astype(BF)
    R = np.zeros((128, 128), dtype=np.float32)
    for h0 in (0, 64):
        for dd in range(32):
            R[h0 + dd, h0 + dd + 32] = -1.0
            R[h0 + dd + 32, h0 + dd] = 1.0
    c["rotT"] = R.T.copy().astype(BF)
    Q = np.zeros((128, 8, 16), dtype=np.float32)
    for j in range(8):
        Q[0:64, j, 2 * j] = 1.0
        Q[64:128, j, 2 * j + 1] = 1.0
    c["sumsq_lhsT"] = Q.reshape(128, 128).astype(BF)
    Bc = np.zeros((16, 8, 128), dtype=np.float32)
    for j in range(8):
        Bc[2 * j, j, 0:64] = 1.0
        Bc[2 * j + 1, j, 64:128] = 1.0
    c["bcast_lhsT"] = Bc.reshape(16, 1024)
    c["ones_col"] = np.ones((128, 1), dtype=BF)
    sk = np.zeros((16, 16, 64), dtype=np.float32)
    for h in range(16):
        sk[h, h, :] = 1.0
    c["selk16"] = sk.reshape(16, 1024)
    ls = np.zeros((16, 16, 128), dtype=np.float32)
    for cc in range(16):
        ls[cc, cc, :] = 1.0
    c["lnsel"] = ls.reshape(16, 2048)
    ns = np.zeros((16, 16, 128), dtype=np.float32)
    for h in range(16):
        ns[h, h, :] = -1.0
    c["negsel16"] = ns.reshape(16, 2048).astype(BF)
    c["one1f"] = np.ones((1, 1), dtype=np.float32)

    c["ones_r128f"] = np.ones((1, 128), dtype=np.float32)
    c["negones_r128"] = np.full((1, 128), -1.0, dtype=BF)
    return c


def tile_km(w):
    K, M = w.shape
    nt = cdiv(K, 128)
    out = np.zeros((128, nt * M), dtype=w.dtype)
    for k in range(nt):
        rows = w[k * 128:(k + 1) * 128]
        out[:rows.shape[0], k * M:(k + 1) * M] = rows
    return out


def tile_vec(v):
    n = cdiv(len(v), 128)
    out = np.zeros((128, n), dtype=v.dtype)
    for j in range(n):
        seg = v[j * 128:(j + 1) * 128]
        out[:len(seg), j] = seg
    return out


class Prog:
    def __init__(self, inputs, debug=(), stages=("A", "B0", "B1", "F"), stop_after=None):
        self.mk = MK(debug=debug)
        self.stages = stages
        self.stop_after = stop_after
        self.C = host_consts()
        self.IN = inputs
        nc = self.mk.nc

        self.x_in = nc.dram_tensor("x_fm", [128, DT * T], F32, kind="ExternalInput")
        self.out_h = nc.dram_tensor("out_fm", [128, DT * T], F32, kind="ExternalOutput")
        self.spill_x = nc.dram_tensor("spill_x", [128, DT * T], F32, kind="Internal")
        self.spill_b = nc.dram_tensor("spill_b", [128, DT * T], F32, kind="Internal")
        self.att_sp = nc.dram_tensor("att_sp", [128, DT * T], BF16, kind="Internal")

        sb = nc.alloc_sbuf_tensor
        self.ident = sb("ident", [128, 128], BF16).ap()
        self.identf = sb("identf", [128, 128], F32).ap()
        self.maskA_cat = sb("maskA_cat", [128, 768], BF16).ap()
        self.maskA_first = sb("maskA_first", [128, 768], BF16).ap()
        self.maskA_last = sb("maskA_last", [128, 768], BF16).ap()
        self.maskB_cat = sb("maskB_cat", [128, 256], BF16).ap()
        self.rotT = sb("rotT", [128, 128], BF16).ap()
        self.sumsq_lhsT = sb("sumsq_lhsT", [128, 128], BF16).ap()
        self.bcast_lhsT = sb("bcast_lhsT", [16, 1024], F32).ap()
        self.ones_col = sb("ones_col", [128, 1], BF16).ap()
        self.one1f = sb("one1f", [1, 1], F32).ap()

        self.ones_r128f = sb("ones_r128f", [1, 128], F32).ap()
        self.negones_r128 = sb("negones_r128", [1, 128], BF16).ap()
        self.c_ap = sb("c_ap", [128, 2], F32).ap()
        self.den16 = sb("den16", [16, T], F32).ap()
        self.selk16 = sb("selk16", [16, 1024], F32).ap()
        self.lnsel = sb("lnsel", [16, 2048], F32).ap()
        self.negsel16 = sb("negsel16", [16, 2048], BF16).ap()
        self.denstage = [sb(f"denstage{i}", [1, CH], F32).ap() for i in range(2)]
        self.ln_tm = sb("ln_tm", [128, 112], F32).ap()
        self.ln_rows = sb("ln_rows", [16, 256], F32).ap()
        self.stat_rows = sb("stat_rows", [1, 2048], F32).ap()
        self.ln_tmp = sb("ln_tmp", [128, CH], F32).ap()
        self.mrow = sb("mrow", [16, 256], BF16).ap()
        self.m8 = sb("m8", [128, 16], BF16).ap()
        self.ob_t = sb("ob_t", [128, 8], F32).ap()
        self.qs_t = sb("qs_t", [128, 2], F32).ap()
        self.ks_t = sb("ks_t", [128, 2], F32).ap()
        self.lng_t = sb("lng_t", [128, 16], F32).ap()
        self.lnb_t = sb("lnb_t", [128, 16], F32).ap()
        self.b1a_t = sb("b1a_t", [128, 44], F32).ap()
        self.b1g_t = sb("b1g_t", [128, 44], F32).ap()
        self.b2_t = sb("b2_t", [128, 16], F32).ap()
        self.unity = sb("unity", [128, 1], F32).ap()

        self.PS = nc.alloc_psum_tensor("PS", [128, 4096], F32).ap()
        self.arena = Arena(nc, "arena", 157696)
        self.x = self.arena.at(0, DT, T, F32)
        self.arena.off = DT * T * 4

        mki = self.mk.input
        IN = inputs
        self.h_qw = mki("w_qw", tile_km(IN["L0_attn_q_w"]))
        self.h_kw = mki("w_kw", tile_km(IN["L0_attn_k_w"]))
        self.h_vw = mki("w_vw", tile_km(IN["L0_attn_v_w"]))
        self.h_ow = mki("w_ow", tile_km(IN["L0_attn_o_w"]))
        self.h_wqkv = [mki(f"w_wqkv{l}", tile_km(IN["L0_ffn_wqkv"][l])) for l in range(2)]
        self.h_wo = [mki(f"w_wo{l}", tile_km(IN["L0_ffn_wo"][l])) for l in range(2)]
        w1, w2 = IN["L0_ffn_w1"], IN["L0_ffn_w2"]
        self.h_w1a = [mki(f"w_w1a{l}", tile_km(w1[l][:, 0:HID])) for l in range(2)]
        self.h_w1g = [mki(f"w_w1g{l}", tile_km(w1[l][:, HID:])) for l in range(2)]
        self.h_w2 = [mki(f"w_w2{l}", tile_km(w2[l])) for l in range(2)]
        ct = host_consts()
        self.h_rot = {nm: mki("c_" + nm, ct[nm]) for nm in ("CQ8", "SQ8", "CK", "SK")}

    def bankap(self, b, rows=128, cols=512, off=0):
        return self.PS[0:rows, 512 * b + off: 512 * b + off + cols]

    def dumpb(self, name, buf, deps):
        ap = buf.full() if isinstance(buf, Buf) else buf
        self.mk.dump(name, ap, deps=deps)

    def emit(self):
        mk = self.mk
        C = self.C
        self.tok_x = mk.dma("y", self.x.full(), self.x_in.ap())
        for nm, dst in (("ident", self.ident), ("identf", self.identf),
                        ("maskA_cat", self.maskA_cat), ("maskA_first", self.maskA_first),
                        ("maskA_last", self.maskA_last),
                        ("maskB_cat", self.maskB_cat),
                        ("rotT", self.rotT), ("sumsq_lhsT", self.sumsq_lhsT),
                        ("bcast_lhsT", self.bcast_lhsT), ("ones_col", self.ones_col),
                        ("one1f", self.one1f), ("selk16", self.selk16),
                        ("lnsel", self.lnsel), ("negsel16", self.negsel16),
                        ("ones_r128f", self.ones_r128f), ("negones_r128", self.negones_r128)):
            mk.input("c_" + nm, C[nm], sbuf=dst)
        IN = self.IN
        mk.input("v_ob", tile_vec(IN["L0_attn_o_b"]), sbuf=self.ob_t)
        mk.input("v_unity", np.full((128, 1), np.float32(IN["L0_attn_unity_scale"]),
                                    dtype=np.float32), sbuf=self.unity)
        mk.input("v_qs", np.stack([np.tile(IN["L0_ffn_q_scale"][l], 2) for l in range(2)],
                                  1).astype(np.float32), sbuf=self.qs_t)
        mk.input("v_ks", np.stack([np.tile(IN["L0_ffn_k_scale"][l], 2) for l in range(2)],
                                  1).astype(np.float32), sbuf=self.ks_t)
        mk.input("v_lng", np.concatenate([tile_vec(IN["L0_ffn_ln_g"][l]) for l in range(2)], 1),
                 sbuf=self.lng_t)
        mk.input("v_lnb", np.concatenate([tile_vec(IN["L0_ffn_ln_b"][l]) for l in range(2)], 1),
                 sbuf=self.lnb_t)
        mk.input("v_b1a", np.concatenate([tile_vec(IN["L0_ffn_b1"][l][0:HID]) for l in range(2)], 1),
                 sbuf=self.b1a_t)
        mk.input("v_b1g", np.concatenate([tile_vec(IN["L0_ffn_b1"][l][HID:]) for l in range(2)], 1),
                 sbuf=self.b1g_t)
        mk.input("v_b2", np.concatenate([tile_vec(IN["L0_ffn_b2"][l]) for l in range(2)], 1),
                 sbuf=self.b2_t)
        self.tok_cons = ("LG", mk.dma_cnt["LG"])

        self.tok_c = mk.step("s", [
            lambda e: e.activation(self.c_ap[:, 1:2], self.unity, AF.Sigmoid),
            lambda e: e.mul(self.c_ap[:, 0:1], self.c_ap[:, 1:2], 0.125),
        ], deps=[self.tok_cons])

        self.bankrot = [0]
        self.tok_last = [self.tok_x]
        try:
            if "A" in self.stages:
                self.emit_stageA()
            for l in range(2):
                if f"B{l}" in self.stages:
                    self.emit_stageB_layer(l)
            if "F" in self.stages:
                self.emit_final()
            else:
                mk.dma("y", self.out_h.ap(), self.x.full(),
                       deps=self.tok_last, store=True)
        except StopEmission:
            pass
        mk.finish()

    def stop_check(self, tag):
        if self.stop_after == tag:
            raise StopEmission()

    def nb(self, lo=0, hi=5):
        self.bankrot[0] = self.bankrot[0] + 1
        return lo + (self.bankrot[0] % (hi - lo))

    def emit_ln(self, src, dep_src, zc, sq, shift=False, gamma=None, beta=None,
                out_bf=None):
        mk = self.mk
        PS = self.PS
        lc = zc.cols
        nlc = T // lc

        def cast_chunk(ch, deps):
            fns = []
            for kt in range(DT):
                if not shift or kt < 4:
                    fns.append(lambda e, kt=kt, ch=ch: e.copy(
                        zc.t(kt), src.t(kt)[:, ch * lc:(ch + 1) * lc]))
                else:
                    if ch == 0:
                        fns.append(lambda e, kt=kt: e.copy(
                            zc.t(kt)[:, 1:lc], src.t(kt)[:, 0:lc - 1]))
                    else:
                        fns.append(lambda e, kt=kt, ch=ch: e.copy(
                            zc.t(kt), src.t(kt)[:, ch * lc - 1:(ch + 1) * lc - 1]))
            tok = mk.step("s", fns, deps=deps)
            if shift and ch == 0:
                tok = mk.step("v", [lambda e, kt=kt: e.memset(zc.t(kt)[:, 0:1], 0.0)
                                    for kt in range(4, 8)], deps=[tok])
            return tok

        stat_tok = []
        prev_mm = {}
        for ch in range(nlc):
            cdeps = list(dep_src)
            if ch - 1 in prev_mm:
                cdeps.append(prev_mm[ch - 1])
            tok_z = cast_chunk(ch, cdeps)
            tok_sq = mk.step("v", [lambda e, kt=kt: e.tensor_mul(
                sq.t(kt), zc.t(kt), zc.t(kt)) for kt in range(DT)], deps=[tok_z])
            fns = []
            for kt in range(DT):
                fns.append(lambda e, kt=kt, ch=ch: e.matmul(
                    PS[0:1, 2048 + lc * ch:2048 + lc * (ch + 1)], self.ones_col,
                    zc.t(kt), start=(kt == 0), stop=(kt == DT - 1)))
            for kt in range(DT):
                fns.append(lambda e, kt=kt, ch=ch: e.matmul(
                    PS[0:1, lc * ch:lc * (ch + 1)], self.ones_col, sq.t(kt),
                    start=(kt == 0), stop=(kt == DT - 1)))
            b0 = (lc * ch) // 512
            b1 = 4 + b0 if lc == 512 else 4 + (lc * ch) // 512
            tokm = mk.step("t", fns, deps=[tok_z, tok_sq, self.tok_cons],
                           banks=[b0, 4 + b0])
            prev_mm[ch] = tokm
            stat_tok.append(tokm)
        tr_tok = []
        for g in range(NCH):
            sl = g % 2
            edeps = [t for t in stat_tok]
            ev = mk.step("s", [
                lambda e, g=g, sl=sl: e.copy(
                    self.stat_rows[0:1, 1024 * sl:1024 * sl + 512],
                    PS[0:1, 2048 + 512 * g:2048 + 512 * (g + 1)]),
                lambda e, g=g, sl=sl: e.copy(
                    self.stat_rows[0:1, 1024 * sl + 512:1024 * sl + 1024],
                    PS[0:1, 512 * g:512 * (g + 1)]),
            ], deps=edeps + ([tr_tok[g - 2]] if g >= 2 else []), banks=[g, 4 + g])
            fns = []
            for j in range(4):
                fns.append(lambda e, g=g, j=j, sl=sl: e.matmul(
                    PS[:, 512 * g + j:512 * g + j + 1],
                    self.stat_rows[0:1, 1024 * sl + 128 * j:1024 * sl + 128 * (j + 1)],
                    self.one1f, start=True, stop=True))
                fns.append(lambda e, g=g, j=j, sl=sl: e.matmul(
                    PS[:, 512 * g + 8 + j:512 * g + 8 + j + 1],
                    self.stat_rows[0:1, 1024 * sl + 512 + 128 * j:
                                   1024 * sl + 512 + 128 * (j + 1)],
                    self.one1f, start=True, stop=True))
            tr_tok.append(mk.step("t", fns, deps=[ev], banks=[g]))

        tm = self.ln_tm
        sum_src = PS[:, 0:2048].rearrange("p (g c) -> p g c", g=4)[:, :, 0:4]
        sq_src = PS[:, 0:2048].rearrange("p (g c) -> p g c", g=4)[:, :, 8:12]
        tok_v = mk.step("v", [
            lambda e: e.tensor_copy(
                tm[:, 0:16].rearrange("p (g c) -> p g c", g=4), sum_src),
            lambda e: e.tensor_copy(
                tm[:, 16:32].rearrange("p (g c) -> p g c", g=4), sq_src),
            lambda e: e.tensor_scalar_mul(tm[:, 32:48], tm[:, 0:16], 1.0 / D),
            lambda e: e.tensor_scalar_mul(tm[:, 48:64], tm[:, 16:32], 1.0 / D),
            lambda e: e.tensor_mul(tm[:, 64:80], tm[:, 32:48], tm[:, 32:48]),
            lambda e: e.tensor_sub(tm[:, 64:80], tm[:, 48:64], tm[:, 64:80]),
            lambda e: e.tensor_scalar_add(tm[:, 64:80], tm[:, 64:80], LN_EPS),
            lambda e: e.reciprocal(tm[:, 64:80], tm[:, 64:80]),
        ], deps=tr_tok, banks=[0, 1, 2, 3])
        tok_s = mk.step("s", [
            lambda e: e.activation(tm[:, 80:96], tm[:, 64:80], AF.Sqrt),
        ], deps=[tok_v])
        tok_v2 = mk.step("v", [
            lambda e: e.tensor_scalar_mul(tm[:, 96:112], tm[:, 32:48], -1.0),
            lambda e: e.tensor_mul(tm[:, 96:112], tm[:, 96:112], tm[:, 80:96]),
        ], deps=[tok_s])
        tok_tr = mk.step("t", [
            lambda e: e.matmul(PS[0:16, 3584:3584 + 128],
                               tm[:, 80:96], self.identf, start=True, stop=True),
            lambda e: e.matmul(PS[0:16, 3584 + 128:3584 + 256],
                               tm[:, 96:112], self.identf, start=True, stop=True),
        ], deps=[tok_v2, self.tok_cons], banks=[7])
        tok_r = mk.step("s", [
            lambda e: e.copy(self.ln_rows[:, 0:256], PS[0:16, 3584:3584 + 256]),
        ], deps=[tok_tr], banks=[7])

        out_toks = []
        ac = 512 if out_bf is None else lc
        nac = T // ac
        for ch in range(nac):
            rg = 1024 * (ch % 2)
            fns = []
            nj = ac // 128
            for j in range(nj):
                c = (ac * ch) // 128 + j
                fns.append(lambda e, c=c, j=j, rg=rg: e.matmul(
                    PS[:, rg + 128 * j:rg + 128 * (j + 1)],
                    self.lnsel[:, 128 * c:128 * (c + 1)],
                    self.ln_rows[:, 0:128], start=True, stop=True))
                fns.append(lambda e, c=c, j=j, rg=rg: e.matmul(
                    PS[:, rg + 512 + 128 * j:rg + 512 + 128 * (j + 1)],
                    self.lnsel[:, 128 * c:128 * (c + 1)],
                    self.ln_rows[:, 128:256], start=True, stop=True))
            bks = [rg // 512, rg // 512 + 1]
            tok_b = mk.step("t", fns, deps=[tok_r], banks=bks)
            if out_bf is None:
                for kt in range(DT):
                    tok = mk.step("v", [
                        lambda e, kt=kt, ch=ch, rg=rg: e.tensor_mul(
                            src.t(kt)[:, ch * ac:(ch + 1) * ac],
                            src.t(kt)[:, ch * ac:(ch + 1) * ac],
                            PS[:, rg:rg + ac]),
                        lambda e, kt=kt, ch=ch, rg=rg: e.tensor_add(
                            src.t(kt)[:, ch * ac:(ch + 1) * ac],
                            src.t(kt)[:, ch * ac:(ch + 1) * ac],
                            PS[:, rg + 512:rg + 512 + ac]),
                    ], deps=[tok_b], banks=bks)
                    out_toks.append(tok)
            else:
                tok_z = cast_chunk(ch, dep_src + [tok_b])
                tok_prev_s = None
                for kt in range(DT):
                    vdeps = [tok_z, tok_b]
                    if tok_prev_s is not None:
                        vdeps.append(tok_prev_s)
                    tokv = mk.step("v", [
                        lambda e, kt=kt, rg=rg: e.tensor_mul(
                            self.ln_tmp[:, 0:ac], zc.t(kt), PS[:, rg:rg + ac]),
                        lambda e, kt=kt, rg=rg: e.tensor_add(
                            self.ln_tmp[:, 0:ac], self.ln_tmp[:, 0:ac],
                            PS[:, rg + 512:rg + 512 + ac]),
                    ], deps=vdeps, banks=bks)
                    tok = mk.step("s", [
                        lambda e, kt=kt, ch=ch: e.activation(
                            out_bf.t(kt)[:, ch * ac:(ch + 1) * ac], self.ln_tmp[:, 0:ac],
                            AF.Identity, bias=beta[:, kt:kt + 1], scale=gamma[:, kt:kt + 1]),
                    ], deps=[tokv])
                    tok_prev_s = tok
                    out_toks.append(tok)
        return out_toks

    def emit_stageA(self):
        mk = self.mk
        A = self.arena
        A.mark()
        xb = A.alloc(DT, T, BF16)
        qf = [A.alloc(1, T, BF16) for _ in range(2)]
        kf = [A.alloc(1, T, BF16) for _ in range(2)]
        qsl = [A.alloc(1, 8 * 128, BF16) for _ in range(2)]
        ksl = [A.alloc(1, 8 * 128, BF16) for _ in range(2)]
        wbig = A.alloc(1, 8 * 1024, BF16)
        attc = [A.alloc(DT, CH, BF16) for _ in range(2)]
        v_aug = A.at(0, TT, 16 * 65, BF16)
        vs_aug = A.at(33280, 4, 16 * 65, BF16)
        es = [A.at(41600 + i * 2048, 1, 1024, BF16) for i in range(2)]
        el = [A.at(45696 + i * 1536, 1, 768, BF16) for i in range(2)]
        att_st = [A.at(48768 + i * 4096, 1, T, BF16) for i in range(2)]
        assert 48768 + 2 * 4096 <= DT * T * 4

        tok_vw = mk.dma("g", wbig.t(0), self.h_vw)

        tok_xb = [mk.step("v", [lambda e, kt=kt: e.tensor_copy(xb.t(kt), self.x.t(kt))],
                          deps=[self.tok_x]) for kt in range(DT)]
        tok_xb_all = tok_xb[-1]

        def ones_ap(buf, tt):
            return buf.t(tt).rearrange("p (h c) -> p h c", h=16)[:, :, 64:65]
        tok_ones = mk.step("v", [lambda e, tt=tt, b=b: e.memset(ones_ap(b, tt), 1.0)
                                 for b in (v_aug, vs_aug) for tt in range(b.tiles)],
                           deps=[tok_xb_all])

        vtoks = []
        for strided in (0, 1):
            buf = (v_aug, vs_aug)[strided]
            for tt in range(buf.tiles):
                for dc in range(2):
                    b = self.nb()
                    fns = []
                    for kt in range(DT):
                        if strided:
                            lhs = xb.t(kt)[:, 512 * tt:512 * (tt + 1):4]
                        else:
                            lhs = xb.t(kt)[:, 128 * tt:128 * (tt + 1)]
                        fns.append(lambda e, kt=kt, dc=dc, b=b, lhs=lhs: e.matmul(
                            self.bankap(b), lhs,
                            wbig.t(0)[:, 1024 * kt + 512 * dc:1024 * kt + 512 * (dc + 1)],
                            start=(kt == 0), stop=(kt == DT - 1)))
                    tokm = mk.step("t", fns, deps=[tok_xb_all, tok_vw], banks=[b])
                    dst = buf.t(tt).rearrange("p (h c) -> p h c", h=16)[:, 8 * dc:8 * (dc + 1), 0:64]
                    src = self.bankap(b).rearrange("p (h c) -> p h c", h=8)
                    vtoks.append(mk.step("v", [lambda e, dst=dst, src=src: e.tensor_copy(dst, src)],
                                         deps=[tokm, tok_ones], banks=[b]))
        tok_v_done = vtoks[-1]
        self.mk.dump("vaug0", v_aug.t(0), deps=[tok_v_done])
        self.mk.dump("vsaug0", vs_aug.t(0), deps=[tok_v_done])

        unit = [0]
        tok_pair = {}
        attn_last = {}
        av_last = {}
        pend_av = None
        attst_tok = {}
        tok_den = []
        den_dma = {}
        for mt in range(8):
            pb = mt % 2
            dep_prev = [tok_pair[mt - 2]] if mt - 2 in tok_pair else []
            tok_lq = mk.dma("g", qsl[pb].t(0).rearrange("p (k m) -> p k m", k=8),
                            self.h_qw.rearrange("p (k m) -> p k m", k=8)[:, :, 128 * mt:128 * (mt + 1)],
                            deps=dep_prev)
            tok_lk = mk.dma("g", ksl[pb].t(0).rearrange("p (k m) -> p k m", k=8),
                            self.h_kw.rearrange("p (k m) -> p k m", k=8)[:, :, 128 * mt:128 * (mt + 1)],
                            deps=dep_prev)
            ptoks = []
            for ch in range(NCH):
                for wsl, dstb, scaled in ((qsl[pb], qf[pb], True), (ksl[pb], kf[pb], False)):
                    b = self.nb()
                    fns = []
                    for kt in range(DT):
                        fns.append(lambda e, kt=kt, ch=ch, b=b, wsl=wsl: e.matmul(
                            self.bankap(b), wsl.t(0)[:, 128 * kt:128 * (kt + 1)],
                            xb.t(kt)[:, CH * ch:CH * (ch + 1)],
                            start=(kt == 0), stop=(kt == DT - 1)))
                    tokm = mk.step("t", fns, deps=[tok_lq, tok_lk, tok_xb_all], banks=[b])
                    gdeps = [attn_last[pb]] if pb in attn_last else []
                    if scaled:
                        tok = mk.step("s", [lambda e, ch=ch, b=b, dstb=dstb: e.activation(
                            dstb.t(0)[:, CH * ch:CH * (ch + 1)], self.bankap(b),
                            AF.Copy, scale=self.c_ap[:, 0:1])],
                            deps=[tokm, self.tok_c] + gdeps, banks=[b])
                    else:
                        tok = mk.step("v", [lambda e, ch=ch, b=b, dstb=dstb: e.tensor_copy(
                            dstb.t(0)[:, CH * ch:CH * (ch + 1)], self.bankap(b))],
                            deps=[tokm] + gdeps, banks=[b])
                    ptoks.append(tok)
            tok_pair[mt] = ptoks[-1]
            if mt == 0:
                self.mk.dump("qf0", qf[0].t(0), deps=ptoks)
                self.mk.dump("kf0", kf[0].t(0), deps=ptoks)

            prev_spill = attst_tok.get(pb)
            for h2 in range(2):
                h = 2 * mt + h2
                r0 = 64 * h2
                for qc in range(8):
                    u = unit[0]
                    unit[0] += 1
                    p = u % 2
                    sreg = 512 * (2 * p)
                    lreg = 2048 + 768 * p
                    lbanks = [4, 5] if p == 0 else [5, 6]
                    ch512 = qc // 2
                    avbank = 7
                    avoff = 256 * (qc % 2)
                    q0 = 256 * qc
                    fns = []
                    for kst in range(4):
                        fns.append(lambda e, kst=kst, pb=pb, r0=r0, q0=q0, sreg=sreg: e.matmul(
                            self.PS[:, sreg + 256 * kst: sreg + 256 * (kst + 1)],
                            kf[pb].t(0)[r0:r0 + 64, 512 * kst:512 * (kst + 1):4],
                            qf[pb].t(0)[r0:r0 + 64, q0:q0 + 256],
                            start=True, stop=True))
                    tok_sc = mk.step("t", fns, deps=ptoks[-2:], banks=[2 * p, 2 * p + 1])
                    fns = []
                    for qi in range(2):
                        qt = 2 * qc + qi
                        for bl in range(3):
                            ktile = min(max(qt - 1 + bl, 0), TT - 1)
                            fns.append(lambda e, pb=pb, r0=r0, qt=qt, qi=qi, bl=bl,
                                       ktile=ktile, lreg=lreg: e.matmul(
                                self.PS[:, lreg + 384 * qi + 128 * bl:
                                        lreg + 384 * qi + 128 * (bl + 1)],
                                kf[pb].t(0)[r0:r0 + 64, 128 * ktile:128 * (ktile + 1)],
                                qf[pb].t(0)[r0:r0 + 64, 128 * qt:128 * (qt + 1)],
                                start=True, stop=True))
                    tok_lc = mk.step("t", fns, deps=ptoks[-2:], banks=lbanks)
                    mask = (self.maskA_first if qc == 0 else
                            (self.maskA_last if qc == 7 else self.maskA_cat))
                    tok_mk = mk.step("v", [lambda e, lreg=lreg, mask=mask: e.tensor_add(
                        self.PS[:, lreg:lreg + 768],
                        self.PS[:, lreg:lreg + 768], mask)],
                        deps=[tok_lc, self.tok_cons], banks=lbanks)
                    edeps = [tok_sc, tok_mk]
                    if p in av_last:
                        edeps.append(av_last[p])
                    tok_e = mk.step("s", [
                        lambda e, sreg=sreg, p=p: e.activation(
                            es[p].t(0), self.PS[:, sreg:sreg + 1024], AF.Exp),
                        lambda e, lreg=lreg, p=p: e.activation(
                            el[p].t(0), self.PS[:, lreg:lreg + 768], AF.Exp),
                    ], deps=edeps, banks=[2 * p, 2 * p + 1] + lbanks)
                    fns = []
                    for kst in range(4):
                        fns.append(lambda e, kst=kst, h=h, p=p, avbank=avbank,
                                   avoff=avoff: e.matmul(
                            self.PS[0:65, 512 * avbank + avoff:512 * avbank + avoff + 256],
                            vs_aug.t(kst)[:, 65 * h:65 * h + 65],
                            es[p].t(0)[:, 256 * kst:256 * (kst + 1)],
                            start=(kst == 0), stop=False))
                    for qi in range(2):
                        qt = 2 * qc + qi
                        for bl in range(3):
                            ktile = min(max(qt - 1 + bl, 0), TT - 1)
                            fns.append(lambda e, qt=qt, qi=qi, bl=bl, ktile=ktile, h=h,
                                       p=p, avbank=avbank, avoff=avoff: e.matmul(
                                self.PS[0:65, 512 * avbank + avoff + 128 * qi:
                                        512 * avbank + avoff + 128 * (qi + 1)],
                                v_aug.t(ktile)[:, 65 * h:65 * h + 65],
                                el[p].t(0)[:, 384 * qi + 128 * bl:384 * qi + 128 * (bl + 1)],
                                start=False, stop=(qi == 1 and bl == 2)))
                    if h == 0 and qc == 0:
                        self.mk.dump("es00", es[p].t(0), deps=[tok_e])
                        self.mk.dump("el00", el[p].t(0), deps=[tok_e])
                    attn_last[pb] = tok_lc

                    def emit_av(fns=fns, tok_e=tok_e, avbank=avbank, p=p, qc=qc,
                                h=h, h2=h2, pb=pb, ch512=ch512,
                                prev_spill=prev_spill):
                        tok_av = mk.step("t", fns, deps=[tok_e, tok_v_done],
                                         banks=[avbank])
                        av_last[p] = tok_av
                        if qc % 2 == 1:
                            evdeps = [tok_av]
                            if prev_spill is not None:
                                evdeps.append(prev_spill)
                            tok_ev = mk.step("v", [lambda e: e.tensor_copy(
                                att_st[pb].t(0)[64 * h2:64 * h2 + 64,
                                                CH * ch512:CH * (ch512 + 1)],
                                self.PS[0:64, 512 * avbank:512 * (avbank + 1)])],
                                deps=evdeps, banks=[avbank])
                            dsb = (h * NCH + ch512) % 2
                            ddeps = [tok_av]
                            if dsb in den_dma:
                                ddeps.append(den_dma[dsb])
                            tok_dn = mk.step("s", [lambda e: e.copy(
                                self.denstage[dsb],
                                self.PS[64:65, 512 * avbank:512 * (avbank + 1)])],
                                deps=ddeps, banks=[avbank])
                            den_dma[dsb] = mk.dma(
                                "y", self.den16[h:h + 1, CH * ch512:CH * (ch512 + 1)],
                                self.denstage[dsb], deps=[tok_dn])
                            tok_den.append(den_dma[dsb])
                            av_last[p] = tok_ev
                    if pend_av is not None:
                        pend_av()
                    pend_av = emit_av
            if pend_av is not None:
                pend_av()
                pend_av = None
            attst_tok[pb] = mk.dma(
                "y", self.att_sp.ap()[:, mt * T * 1:(mt + 1) * T], att_st[pb].t(0),
                deps=[av_last[0], av_last[1]], store=True)
        tok_att_spill = [attst_tok[0], attst_tok[1]]

        tok_r = mk.step("v", [
            lambda e: e.reciprocal(self.den16, self.den16),
        ], deps=tok_den)
        self.mk.dump("den16", self.den16, deps=[tok_r])
        self.mk.dump("r16f", self.den16, deps=[tok_r])

        tok_ow = mk.dma("g", wbig.t(0), self.h_ow, deps=vtoks[-1:])
        tok_xrl = mk.dma("y", self.x.full(), self.x_in.ap(),
                         deps=[av_last[0], av_last[1], tok_ones] + vtoks[-1:] + tok_att_spill)
        tok_res = []
        attrl_tok = {}
        for ch in range(NCH):
            cb = ch % 2
            rdeps = tok_att_spill + ([attrl_tok[cb]] if cb in attrl_tok else [])
            tok_arl = mk.dma("y", attc[cb].full().rearrange("p (k c) -> p k c", k=DT),
                             self.att_sp.ap().rearrange("p (k t) -> p k t", k=DT)
                             [:, :, CH * ch:CH * (ch + 1)], deps=rdeps)
            ntoks = []
            for h in range(NH_A):
                b = 4 + (h % 2)
                tok_b = mk.step("t", [
                    lambda e, h=h, ch=ch, b=b: e.matmul(
                        self.PS[0:64, 512 * b:512 * b + CH],
                        self.selk16[:, 64 * h:64 * (h + 1)],
                        self.den16[:, CH * ch:CH * (ch + 1)], start=True, stop=True),
                ], deps=[tok_r, tok_arl], banks=[b])
                ntoks.append(mk.step("v", [lambda e, h=h, cb=cb, b=b: e.tensor_mul(
                    attc[cb].t(h // 2)[64 * (h % 2):64 * (h % 2) + 64, :],
                    attc[cb].t(h // 2)[64 * (h % 2):64 * (h % 2) + 64, :],
                    self.PS[0:64, 512 * b:512 * b + CH])],
                    deps=[tok_b, tok_arl], banks=[b]))
            for mt in range(DT):
                b = self.nb(0, 4)
                fns = []
                for kt in range(DT):
                    fns.append(lambda e, kt=kt, mt=mt, cb=cb, b=b: e.matmul(
                        self.bankap(b), wbig.t(0)[:, 1024 * kt + 128 * mt:1024 * kt + 128 * (mt + 1)],
                        attc[cb].t(kt), start=(kt == 0), stop=(kt == DT - 1)))
                tokm = mk.step("t", fns, deps=[tok_ow] + ntoks, banks=[b])
                tokv = mk.step("v", [lambda e, mt=mt, ch=ch, b=b: e.tensor_add(
                    self.x.t(mt)[:, CH * ch:CH * (ch + 1)],
                    self.x.t(mt)[:, CH * ch:(ch + 1) * CH], self.bankap(b))],
                    deps=[tokm, tok_xrl], banks=[b])
                toks = mk.step("s", [lambda e, mt=mt, ch=ch: e.activation(
                    self.x.t(mt)[:, CH * ch:(ch + 1) * CH],
                    self.x.t(mt)[:, CH * ch:(ch + 1) * CH],
                    AF.Identity, bias=self.ob_t[:, mt:mt + 1])],
                    deps=[tokv])
                tok_res.append(toks)
            if ch == 0:
                self.mk.dump("attn0", attc[0].full(), deps=ntoks)
            attrl_tok[cb] = tokm

        self.mk.dump("zres", self.x.full(), deps=tok_res)
        A.reset()
        A.mark()
        zc = A.alloc(DT, CH, BF16)
        sq = A.alloc(DT, CH, BF16)
        ln_toks = self.emit_ln(self.x, tok_res, zc, sq)
        A.reset()
        self.tok_last = ln_toks
        self.tok_spill_b = mk.dma("y", self.spill_b.ap(), self.x.full(),
                                  deps=ln_toks, store=True)
        self.dumpb("xA", self.x, ln_toks)

    def emit_stageB_layer(self, l):
        mk = self.mk
        A = self.arena
        PS = self.PS
        A.mark()
        h_sh = A.at(0, DT, T, BF16)
        tabs = A.at(32768, 2, T, BF16)
        EB = [A.at(40960 + i * 1280, 1, 640, BF16) for i in range(2)]
        v_augB = A.at(43520, TT, 8 * 65, BF16)
        qraw = A.alloc(4, T, BF16)
        kraw = A.alloc(4, T, BF16)
        q8 = A.alloc(4, T, BF16)
        attB = A.alloc(4, T, BF16)
        wo_sb = A.alloc(1, 4 * 1024, BF16)
        wv_sb = A.alloc(1, 8 * 512, BF16)
        wsl = [A.alloc(1, 8 * 128, BF16) for _ in range(2)]
        sqb = [A.alloc(1, T, BF16)]
        tmpb = [A.alloc(1, CH, BF16) for _ in range(2)]
        kr = qraw

        wqkv_r = self.h_wqkv[l].rearrange("p (k m) -> p k m", k=DT)

        tok_spx = mk.dma("y", self.spill_x.ap(), self.x.full(),
                         deps=self.tok_last, store=True)
        tok_h = []
        for kt in range(DT):
            if kt < 4:
                tok_h.append(mk.step("s", [lambda e, kt=kt: e.copy(
                    h_sh.t(kt), self.x.t(kt))], deps=self.tok_last + [tok_spx]))
            else:
                tok_h.append(mk.step("s", [lambda e, kt=kt: e.copy(
                    h_sh.t(kt)[:, 1:T], self.x.t(kt)[:, 0:T - 1])],
                    deps=self.tok_last + [tok_spx]))
        tok_h.append(mk.step("v", [lambda e, kt=kt: e.memset(h_sh.t(kt)[:, 0:1], 0.0)
                                   for kt in range(4, 8)], deps=tok_h))
        tok_h_all = tok_h[-1]
        if l == 0:
            self.mk.dump("hB0", h_sh.full(), deps=[tok_h_all])

        vdep = mk.dma("g", wv_sb.t(0).rearrange("p (k m) -> p k m", k=DT),
                      wqkv_r[:, :, 1024:1536], deps=self.tok_last)
        ptoks = []
        proj_done = {}
        for which in range(2):
            dst = (qraw, kraw)[which]
            for mt in range(4):
                pb = (which * 4 + mt) % 2
                ldeps = [proj_done[pb]] if pb in proj_done else list(self.tok_last)
                tok_l = mk.dma("g", wsl[pb].t(0).rearrange("p (k m) -> p k m", k=DT),
                               wqkv_r[:, :, 512 * which + 128 * mt:
                                      512 * which + 128 * (mt + 1)], deps=ldeps)
                if l == 0 and which == 0 and mt == 0:
                    self.mk.dump("wsl0", wsl[0].t(0), deps=[tok_l])
                    self.stop_check("wsl")
                for ch in range(NCH):
                    b = self.nb(0, 4)
                    fns = []
                    for kt in range(DT):
                        fns.append(lambda e, kt=kt, ch=ch, b=b, pb=pb: e.matmul(
                            self.bankap(b), wsl[pb].t(0)[:, 128 * kt:128 * (kt + 1)],
                            h_sh.t(kt)[:, CH * ch:CH * (ch + 1)],
                            start=(kt == 0), stop=(kt == DT - 1)))
                    tokm = mk.step("t", fns, deps=[tok_l, tok_h_all], banks=[b])
                    ptoks.append(mk.step("v", [lambda e, ch=ch, b=b, dst=dst, mt=mt:
                                               e.tensor_copy(
                        dst.t(mt)[:, CH * ch:CH * (ch + 1)], self.bankap(b))],
                        deps=[tokm], banks=[b]))
                proj_done[pb] = tokm
        tok_qk = ptoks[-1]
        if l == 0:
            self.mk.dump("qraw0", qraw.t(0), deps=[tok_qk])
            self.mk.dump("kraw0", kraw.t(0), deps=[tok_qk])
            self.mk.dump("wsl0b", wsl[0].t(0), deps=[tok_qk])
            self.stop_check("qraw")
        tok_vones = mk.step("v", [lambda e, tt=tt: e.memset(
            v_augB.t(tt).rearrange("p (h c) -> p h c", h=8)[:, :, 64:65], 1.0)
            for tt in range(TT)], deps=self.tok_last + [tok_spx])
        vtoks = []
        for tt in range(TT):
            b = self.nb(0, 4)
            fns = []
            for kt in range(DT):
                fns.append(lambda e, kt=kt, tt=tt, b=b: e.matmul(
                    self.bankap(b), h_sh.t(kt)[:, 128 * tt:128 * (tt + 1)],
                    wv_sb.t(0)[:, 512 * kt:512 * (kt + 1)],
                    start=(kt == 0), stop=(kt == DT - 1)))
            tokm = mk.step("t", fns, deps=[vdep, tok_h_all], banks=[b])
            dst = v_augB.t(tt).rearrange("p (h c) -> p h c", h=8)[:, :, 0:64]
            src = self.bankap(b).rearrange("p (h c) -> p h c", h=8)
            vtoks.append(mk.step("v", [lambda e, dst=dst, src=src: e.tensor_copy(dst, src)],
                                 deps=[tokm, tok_vones], banks=[b]))
        tok_v_done = vtoks[-1]

        mm_tok = {}
        sq_tok = {}
        for j in range(8):
            src = (qraw, kraw)[j // 4]
            tl = j % 4
            sdeps = [tok_qk]
            if j - 1 in mm_tok:
                sdeps.append(mm_tok[j - 1])
            sq_tok[j] = mk.step("s", [lambda e, src=src, tl=tl, j=j: e.activation(
                sqb[0].t(0), src.t(tl), AF.Square)], deps=sdeps)
            fns = []
            for ch in range(NCH):
                fns.append(lambda e, j=j, ch=ch: e.matmul(
                    PS[0:16, 2048 + CH * ch:2048 + CH * (ch + 1)],
                    self.sumsq_lhsT[:, 16 * j:16 * (j + 1)],
                    sqb[0].t(0)[:, CH * ch:CH * (ch + 1)],
                    start=(j == 0), stop=(j == 7)))
            mm_tok[j] = mk.step("t", fns, deps=[sq_tok[j]], banks=[4, 5, 6, 7])
        tok_rb = mk.step("v", [
            lambda e: e.tensor_scalar_max(self.den16, PS[0:16, 2048:4096], 1e-24),
            lambda e: e.reciprocal(self.den16, self.den16),
        ], deps=[mm_tok[7]], banks=[4, 5, 6, 7])
        tok_rb = mk.step("s", [
            lambda e: e.activation(self.den16, self.den16, AF.Sqrt),
        ], deps=[tok_rb])
        if l == 0:
            self.mk.dump("rbB", self.den16, deps=[tok_rb])
            self.stop_check("rb")
        hat_tok = {}
        for j in range(8):
            src = (qraw, kraw)[j // 4]
            tl = j % 4
            sc = (self.qs_t, self.ks_t)[j // 4][:, l:l + 1]
            for ch in range(NCH):
                b = self.nb(0, 4)
                tok_b = mk.step("t", [lambda e, j=j, ch=ch, b=b: e.matmul(
                    self.bankap(b), self.bcast_lhsT[:, 128 * j:128 * (j + 1)],
                    self.den16[:, CH * ch:CH * (ch + 1)], start=True, stop=True)],
                    deps=[tok_rb], banks=[b])
                hat_tok[(j, ch)] = mk.step("v", [lambda e, src=src, tl=tl, ch=ch, b=b,
                                                 sc=sc: e.scalar_tensor_tensor(
                    src.t(tl)[:, CH * ch:CH * (ch + 1)],
                    src.t(tl)[:, CH * ch:CH * (ch + 1)], sc,
                    self.bankap(b), OP.mult, OP.mult)],
                    deps=[tok_b], banks=[b])
        tok_hat = list(hat_tok.values())[-1]

        def rotary(srcb, dstb, cnm, snm, deps):
            tok_tab = mk.dma("g", tabs.t(0), self.h_rot[cnm], deps=deps + [tok_spx])
            tok_tab2 = mk.dma("g", tabs.t(1), self.h_rot[snm], deps=deps + [tok_spx])
            outs = []
            for tl in range(4):
                for ch in range(NCH):
                    b = self.nb(0, 4)
                    tok_rot = mk.step("t", [lambda e, srcb=srcb, tl=tl, ch=ch, b=b: e.matmul(
                        self.bankap(b), self.rotT,
                        srcb.t(tl)[:, CH * ch:CH * (ch + 1)], start=True, stop=True)],
                        deps=[tok_tab, tok_tab2] + deps, banks=[b])
                    sl = (4 * tl + ch) % 2
                    t1 = mk.step("v", [lambda e, srcb=srcb, tl=tl, ch=ch, sl=sl: e.tensor_mul(
                        tmpb[sl].t(0),
                        srcb.t(tl)[:, CH * ch:CH * (ch + 1)],
                        tabs.t(0)[:, CH * ch:CH * (ch + 1)])],
                        deps=[tok_tab, tok_tab2] + deps)
                    t2 = mk.step("v", [lambda e, dstb=dstb, tl=tl, ch=ch, b=b: e.tensor_mul(
                        dstb.t(tl)[:, CH * ch:CH * (ch + 1)],
                        self.bankap(b),
                        tabs.t(1)[:, CH * ch:CH * (ch + 1)])],
                        deps=[tok_rot], banks=[b])
                    t3 = mk.step("v", [lambda e, dstb=dstb, tl=tl, ch=ch, sl=sl: e.tensor_add(
                        dstb.t(tl)[:, CH * ch:CH * (ch + 1)],
                        dstb.t(tl)[:, CH * ch:CH * (ch + 1)],
                        tmpb[sl].t(0))],
                        deps=[t1, t2])
                    outs.append(t3)
            return outs

        if l == 0:
            self.mk.dump("qhat0", qraw.t(0), deps=[tok_hat])
            self.stop_check("qhat")
        rq = rotary(qraw, q8, "CQ8", "SQ8", [tok_hat])
        rk = rotary(kraw, kr, "CK", "SK", [rq[-1]])
        tok_rot_all = rk[-1]

        if l == 0:
            self.mk.dump("q8r0", q8.t(0), deps=[tok_rot_all])
            self.mk.dump("kr0", kr.t(0), deps=[tok_rot_all])
            self.stop_check("rot")
        av_last = {}
        eb_last = {}
        ks_last = {}
        tok_den = []
        den_dma = {}
        attB_ev = []
        pend_av = None
        prev_mt = None
        mstage = self.mrow
        u = 0
        for h in range(NH_B):
            tl, r0 = h // 2, 64 * (h % 2)
            hp = h % 2
            dtoks = []
            for qt in range(TT):
                dbank = 6 + (qt % 2)
                dcol = 512 * dbank + 128 * ((qt % 8) // 2)
                tok_d = mk.step("t", [lambda e, tl=tl, r0=r0, qt=qt, dcol=dcol: e.matmul(
                    PS[:, dcol:dcol + 128],
                    q8.t(tl)[r0:r0 + 64, 128 * qt:128 * (qt + 1)],
                    kr.t(tl)[r0:r0 + 64, 128 * qt:128 * (qt + 1)],
                    start=True, stop=True)],
                    deps=[tok_rot_all], banks=[6 + (qt % 2)])
                rdeps = [tok_d]
                if prev_mt is not None:
                    rdeps.append(prev_mt)
                dtoks.append(mk.step("v", [lambda e, qt=qt, dcol=dcol: e.reduce_max(
                    self.m8[:, qt:qt + 1], PS[:, dcol:dcol + 128], axis=AX)],
                    deps=rdeps, banks=[6 + (qt % 2)]))
            msdeps = list(dtoks)
            if hp in ks_last:
                msdeps.append(ks_last[hp])
            tok_mt = mk.step("t", [lambda e: e.matmul(
                PS[0:16, 3456:3584], self.m8[:, 0:16], self.ident,
                start=True, stop=True)], deps=msdeps, banks=[6])
            tok_ms = mk.step("s", [lambda e, hp=hp: e.copy(
                mstage[:, 128 * hp:128 * (hp + 1)], PS[0:16, 3456:3584])],
                deps=[tok_mt], banks=[6])
            prev_mt = tok_mt

            for qt in range(TT):
                p = u % 2
                u += 1
                ksreg = (0, 1024)[p]
                ksbanks = ([0, 1], [2, 3])[p]
                nb_ = min(qt, 4) + 1
                kt0 = qt - (nb_ - 1)
                fns = []
                for bi in range(nb_):
                    kt = kt0 + bi
                    fns.append(lambda e, tl=tl, r0=r0, kt=kt, qt=qt, bi=bi,
                               ksreg=ksreg: e.matmul(
                        PS[:, ksreg + 128 * bi:ksreg + 128 * (bi + 1)],
                        kr.t(tl)[r0:r0 + 64, 128 * kt:128 * (kt + 1)],
                        q8.t(tl)[r0:r0 + 64, 128 * qt:128 * (qt + 1)],
                        start=True, stop=False))
                    fns.append(lambda e, qt=qt, bi=bi, ksreg=ksreg, hp=hp: e.matmul(
                        PS[:, ksreg + 128 * bi:ksreg + 128 * (bi + 1)],
                        self.negsel16[:, 128 * qt:128 * (qt + 1)],
                        mstage[:, 128 * hp:128 * (hp + 1)],
                        start=False, stop=True))
                tok_ks = mk.step("t", fns, deps=[tok_ms, tok_rot_all], banks=ksbanks)
                ks_last[hp] = tok_ks
                if qt >= 4:
                    mk_in0 = PS[:, ksreg:ksreg + 640].rearrange(
                        "p (a b) -> p a b", a=5)[:, 0:5:4, :]
                    mk_in1 = self.maskB_cat.rearrange("p (a b) -> p a b", a=2)
                    tok_mk = mk.step("v", [lambda e, mk_in0=mk_in0, mk_in1=mk_in1:
                                           e.tensor_tensor(mk_in0, mk_in0, mk_in1, OP.add)],
                                     deps=[tok_ks, self.tok_cons], banks=ksbanks)
                else:
                    lastoff = ksreg + 128 * (nb_ - 1)
                    tok_mk = mk.step("v", [lambda e, lastoff=lastoff: e.tensor_add(
                        PS[:, lastoff:lastoff + 128],
                        PS[:, lastoff:lastoff + 128],
                        self.maskB_cat[:, 128:256])],
                        deps=[tok_ks, self.tok_cons], banks=ksbanks)
                edeps = [tok_mk, tok_spx]
                if p in eb_last:
                    edeps.append(eb_last[p])
                tok_e = mk.step("s", [lambda e, ksreg=ksreg, nb_=nb_, p=p: e.activation(
                    EB[p].t(0)[:, 0:128 * nb_], PS[:, ksreg:ksreg + 128 * nb_], AF.Exp)],
                    deps=edeps, banks=ksbanks)
                ch512 = qt // 4
                avreg = 2048 + 512 * (ch512 % 2)
                avoff = 128 * (qt % 4)
                fns = []
                for bi in range(nb_):
                    kt = kt0 + bi
                    fns.append(lambda e, kt=kt, h=h, p=p, bi=bi, avreg=avreg,
                               avoff=avoff, nb_=nb_: e.matmul(
                        PS[0:65, avreg + avoff:avreg + avoff + 128],
                        v_augB.t(kt)[:, 65 * h:65 * h + 65],
                        EB[p].t(0)[:, 128 * bi:128 * (bi + 1)],
                        start=(bi == 0), stop=(bi == nb_ - 1)))

                def emit_av(fns=fns, tok_e=tok_e, p=p, qt=qt, h=h, ch512=ch512,
                            avreg=avreg):
                    tok_av = mk.step("t", fns, deps=[tok_e, tok_v_done],
                                     banks=[4 + (ch512 % 2)])
                    eb_last[p] = tok_av
                    if qt % 4 == 3:
                        tok_ev = mk.step("v", [lambda e: e.tensor_copy(
                            attB.t(h // 2)[64 * (h % 2):64 * (h % 2) + 64,
                                           CH * ch512:CH * (ch512 + 1)],
                            PS[0:64, avreg:avreg + 512])],
                            deps=[tok_av], banks=[4 + (ch512 % 2)])
                        dsb = (h + ch512) % 2
                        ddeps = [tok_av] + ([den_dma[dsb]] if dsb in den_dma else [])
                        tok_dn = mk.step("s", [lambda e: e.copy(
                            self.denstage[dsb], PS[64:65, avreg:avreg + 512])],
                            deps=ddeps, banks=[4 + (ch512 % 2)])
                        den_dma[dsb] = mk.dma(
                            "y", self.den16[h:h + 1, CH * ch512:CH * (ch512 + 1)],
                            self.denstage[dsb], deps=[tok_dn])
                        tok_den.append(den_dma[dsb])
                        attB_ev.append(tok_ev)
                        eb_last[p] = tok_ev
                if pend_av is not None:
                    pend_av()
                pend_av = emit_av
        if pend_av is not None:
            pend_av()
            pend_av = None

        tok_r = mk.step("v", [
            lambda e: e.reciprocal(self.den16[0:8, :], self.den16[0:8, :]),
        ], deps=tok_den + attB_ev)
        ntoks = []
        for h in range(NH_B):
            for ch in range(NCH):
                b = 6 + (h % 2)
                tok_b = mk.step("t", [lambda e, h=h, ch=ch, b=b: e.matmul(
                    self.PS[0:64, 512 * b:512 * b + CH],
                    self.selk16[:, 64 * h:64 * (h + 1)],
                    self.den16[:, CH * ch:CH * (ch + 1)], start=True, stop=True)],
                    deps=[tok_r], banks=[b])
                ntoks.append(mk.step("v", [lambda e, h=h, ch=ch, b=b: e.tensor_mul(
                    attB.t(h // 2)[64 * (h % 2):64 * (h % 2) + 64, CH * ch:CH * (ch + 1)],
                    attB.t(h // 2)[64 * (h % 2):64 * (h % 2) + 64, CH * ch:CH * (ch + 1)],
                    self.PS[0:64, 512 * b:512 * b + CH])],
                    deps=[tok_b], banks=[b]))

        if l == 0:
            self.mk.dump("denB", self.den16, deps=[tok_r])
            self.mk.dump("attB0", attB.full(), deps=ntoks)
            self.stop_check("att")
        tok_wo = mk.dma("g", wo_sb.t(0), self.h_wo[l], deps=self.tok_last)
        tok_xrl = mk.dma("y", self.x.full(), self.spill_x.ap(),
                         deps=[tok_spx, tok_rot_all, tok_v_done, tok_h_all] + attB_ev)
        tok_res = []
        for ch in range(NCH):
            for mt in range(DT):
                b = self.nb(0, 4)
                fns = []
                for kt in range(4):
                    fns.append(lambda e, kt=kt, mt=mt, ch=ch, b=b: e.matmul(
                        self.bankap(b),
                        wo_sb.t(0)[:, 1024 * kt + 128 * mt:1024 * kt + 128 * (mt + 1)],
                        attB.t(kt)[:, CH * ch:CH * (ch + 1)],
                        start=(kt == 0), stop=(kt == 3)))
                tokm = mk.step("t", fns, deps=[tok_wo] + ntoks, banks=[b])
                tok_res.append(mk.step("v", [lambda e, mt=mt, ch=ch, b=b: e.tensor_add(
                    self.x.t(mt)[:, CH * ch:CH * (ch + 1)],
                    self.x.t(mt)[:, CH * ch:CH * (ch + 1)], self.bankap(b))],
                    deps=[tokm, tok_xrl], banks=[b]))
        A.reset()

        if l == 0:
            self.mk.dump("xwo", self.x.full(), deps=tok_res)
            self.stop_check("xwo")
        A.mark()
        hln = A.alloc(DT, T, BF16)
        w1a_sb = [A.alloc(1, 8 * 512, BF16) for _ in range(2)]
        w1g_sb = [A.alloc(1, 8 * 512, BF16) for _ in range(2)]
        w2_sb = [A.alloc(1, 4 * 1024, BF16) for _ in range(2)]
        prodgg_base = A.off
        prod = A.alloc(8, CH, BF16)
        gg = A.alloc(2, CH, BF16)
        zc = A.at(prodgg_base, DT, 256, BF16)
        sq = A.at(prodgg_base + DT * 512, DT, 256, BF16)
        ln_toks = self.emit_ln(self.x, tok_res, zc, sq, shift=True,
                               gamma=self.lng_t[:, 8 * l:8 * (l + 1)],
                               beta=self.lnb_t[:, 8 * l:8 * (l + 1)],
                               out_bf=hln)

        phases = [(0, 512, 4, 128), (512, 512, 4, 128), (1024, 512, 4, 128),
                  (1536, 512, 4, 128), (2048, 384, 3, 128), (2432, 298, 3, 42)]
        tok_b2 = []
        for kt in range(DT):
            tok_b2.append(mk.step("s", [lambda e, kt=kt: e.activation(
                self.x.t(kt), self.x.t(kt), AF.Identity,
                bias=self.b2_t[:, 8 * l + kt:8 * l + kt + 1])],
                deps=ln_toks + tok_res))

        w2b_prev = None
        slot_free = {}
        load_tok = {}
        w1a_r = self.h_w1a[l].rearrange("p (k m) -> p k m", k=DT)
        w1g_r = self.h_w1g[l].rearrange("p (k m) -> p k m", k=DT)
        ffn_out = []

        def emit_load(pi):
            off, ncols, ntl, lastrows = phases[pi]
            sl = pi % 2
            toff = off // 128
            ldeps = [slot_free[sl]] if sl in slot_free else [tok_res[-1], ln_toks[-1]]
            tok_wa = mk.dma("g", w1a_sb[sl].t(0).rearrange("p (k m) -> p k m", k=DT)
                            [:, :, 0:ncols], w1a_r[:, :, off:off + ncols], deps=ldeps)
            tok_wg = mk.dma("g", w1g_sb[sl].t(0).rearrange("p (k m) -> p k m", k=DT)
                            [:, :, 0:ncols], w1g_r[:, :, off:off + ncols], deps=ldeps)
            tok_w2 = mk.dma("g", w2_sb[sl].t(0)[:, 0:ntl * 1024],
                            self.h_w2[l][:, 1024 * toff:1024 * (toff + ntl)], deps=ldeps)
            load_tok[pi] = (tok_wa, tok_wg, tok_w2)

        emit_load(0)
        emit_load(1)
        for pi, (off, ncols, ntl, lastrows) in enumerate(phases):
            sl = pi % 2
            toff = off // 128
            tok_wa, tok_wg, tok_w2 = load_tok[pi]
            last_mm = None
            for ch in range(NCH):
                prod_toks = {}
                prod_prev = None
                pend_w2a = None
                for j in range(ntl):
                    rows = lastrows if j == ntl - 1 else 128
                    jb = j % 2
                    areg = 2048 + 1024 * jb
                    greg = areg + 512
                    fns = []
                    for kt in range(DT):
                        fns.append(lambda e, kt=kt, j=j, ch=ch, rows=rows, areg=areg,
                                   sl=sl: e.matmul(
                            PS[0:rows, areg:areg + CH],
                            w1a_sb[sl].t(0).rearrange("p (k m) -> p k m", k=DT)
                            [:, kt, 128 * j:128 * j + rows],
                            hln.t(kt)[:, CH * ch:CH * (ch + 1)],
                            start=(kt == 0), stop=(kt == DT - 1)))
                    for kt in range(DT):
                        fns.append(lambda e, kt=kt, j=j, ch=ch, rows=rows, greg=greg,
                                   sl=sl: e.matmul(
                            PS[0:rows, greg:greg + CH],
                            w1g_sb[sl].t(0).rearrange("p (k m) -> p k m", k=DT)
                            [:, kt, 128 * j:128 * j + rows],
                            hln.t(kt)[:, CH * ch:CH * (ch + 1)],
                            start=(kt == 0), stop=(kt == DT - 1)))
                    tok_ag = mk.step("t", fns, deps=[tok_wa, tok_wg] + ln_toks,
                                     banks=[4 + 2 * jb, 5 + 2 * jb])
                    jg = toff + j
                    ggdeps = [tok_ag]
                    if prod_prev is not None:
                        ggdeps.append(prod_prev)
                    tok_gg = mk.step("s", [lambda e, rows=rows, jg=jg, jb=jb, greg=greg:
                                           e.activation(
                        gg.t(jb)[0:rows, :], PS[0:rows, greg:greg + CH], AF.Gelu,
                        bias=self.b1g_t[0:rows, 22 * l + jg:22 * l + jg + 1])],
                        deps=ggdeps, banks=[5 + 2 * jb])
                    pdeps = [tok_gg]
                    if w2b_prev is not None:
                        pdeps.append(w2b_prev)
                    tok_p = mk.step("v", [lambda e, rows=rows, jg=jg, j=j, jb=jb,
                                          areg=areg: e.scalar_tensor_tensor(
                        prod.t(j)[0:rows, :], PS[0:rows, areg:areg + CH],
                        self.b1a_t[0:rows, 22 * l + jg:22 * l + jg + 1],
                        gg.t(jb)[0:rows, :], OP.add, OP.mult)],
                        deps=pdeps, banks=[4 + 2 * jb])
                    prod_toks[j] = tok_p
                    prod_prev = tok_p

                    def emit_w2a(j=j, rows=rows, tok_p=tok_p, sl=sl, ntl=ntl,
                                 tok_w2=tok_w2):
                        fns = []
                        for mt in range(4):
                            fns.append(lambda e, j=j, mt=mt, rows=rows: e.matmul(
                                PS[:, CH * mt:CH * (mt + 1)],
                                w2_sb[sl].t(0)[0:rows, 1024 * j + 128 * mt:
                                               1024 * j + 128 * (mt + 1)],
                                prod.t(j)[0:rows, :],
                                start=(j == 0), stop=(j == ntl - 1)))
                        return mk.step("t", fns, deps=[tok_p, tok_w2],
                                       banks=[0, 1, 2, 3])
                    if pend_w2a is not None:
                        tok_w2a = pend_w2a()
                    pend_w2a = emit_w2a
                tok_w2a = pend_w2a()
                fns = []
                for j in range(ntl):
                    rows = lastrows if j == ntl - 1 else 128
                    for mt in range(4, 8):
                        fns.append(lambda e, j=j, mt=mt, rows=rows, sl=sl: e.matmul(
                            PS[:, 2048 + CH * (mt - 4):2048 + CH * (mt - 3)],
                            w2_sb[sl].t(0)[0:rows, 1024 * j + 128 * mt:
                                           1024 * j + 128 * (mt + 1)],
                            prod.t(j)[0:rows, :],
                            start=(j == 0), stop=(j == ntl - 1)))
                tok_w2b = mk.step("t", fns, deps=[prod_toks[ntl - 1], tok_w2],
                                  banks=[4, 5, 6, 7])
                w2b_prev = tok_w2b
                last_mm = tok_w2b
                xlo = self.x.ap_fn(0, span=4).rearrange(
                    "p (m t) -> p m t", m=4)[:, :, CH * ch:CH * (ch + 1)]
                xhi = self.x.ap_fn(4, span=4).rearrange(
                    "p (m t) -> p m t", m=4)[:, :, CH * ch:CH * (ch + 1)]
                pslo = PS[:, 0:2048].rearrange("p (m t) -> p m t", m=4)
                pshi = PS[:, 2048:4096].rearrange("p (m t) -> p m t", m=4)
                ffn_out.append(mk.step("v", [lambda e, xlo=xlo, pslo=pslo: e.tensor_add(
                    xlo, xlo, pslo)], deps=[tok_w2a] + tok_b2, banks=[0, 1, 2, 3]))
                ffn_out.append(mk.step("v", [lambda e, xhi=xhi, pshi=pshi: e.tensor_add(
                    xhi, xhi, pshi)], deps=[tok_w2b] + tok_b2, banks=[4, 5, 6, 7]))
            slot_free[sl] = last_mm
            if pi + 2 < len(phases):
                emit_load(pi + 2)
        A.reset()

        self.tok_last = ffn_out
    def emit_final(self):
        mk = self.mk
        A = self.arena
        A.mark()
        xc = A.alloc(DT, T, F32)
        tok_rl = mk.dma("y", xc.full(), self.spill_b.ap(),
                        deps=[self.tok_spill_b] + self.tok_last)
        tok_add = []
        for kt in range(DT):
            tok_add.append(mk.step("v", [lambda e, kt=kt: e.tensor_add(
                self.x.t(kt), self.x.t(kt), xc.t(kt))],
                deps=[tok_rl] + self.tok_last))
        zc = A.alloc(DT, CH, BF16)
        sq = A.alloc(DT, CH, BF16)
        ln_toks = self.emit_ln(self.x, tok_add, zc, sq)
        A.reset()
        mk.dma("y", self.out_h.ap(), self.x.full(), deps=ln_toks, store=True)


def _prep_x(x):
    return np.ascontiguousarray(
        x.T.reshape(DT, 128, T).transpose(1, 0, 2).reshape(128, DT * T))


def _unprep_out(o):
    return o.reshape(128, DT, T).transpose(1, 0, 2).reshape(D, T).T


def build(inputs, debug=(), stages=("A", "B0", "B1", "F"), stop_after=None):
    p = Prog(inputs, debug=debug, stages=stages, stop_after=stop_after)
    p.emit()
    return p


def kernel(**inputs):
    inputs = {k: np.asarray(v, dtype=np.float32) for k, v in inputs.items()}
    p = build(inputs)
    x = inputs["L0_attn_x"]
    in_maps = []
    for c in range(8):
        m = dict(p.mk.inputs)
        m["x_fm"] = _prep_x(x[c])
        in_maps.append(m)
    res = run_bass_kernel_spmd(p.mk.nc, in_maps, core_ids=list(range(8)))
    out = np.stack([_unprep_out(res.results[c]["out_fm"]) for c in range(8)])
    return out.astype(np.float32)
